# version 17
# baseline (speedup 1.0000x reference)
import numpy as np
import ml_dtypes
from contextlib import ExitStack

import concourse.bass as bass
import concourse.mybir as mybir
from concourse.bass_utils import run_bass_kernel_spmd

F32 = mybir.dt.float32
BF16 = mybir.dt.bfloat16
OP = mybir.AluOpType
AF = mybir.ActivationFunctionType
AX = mybir.AxisListType.X

B, S, D = 4, 2048, 2048
NH, DH = 16, 128
FF = 5632
TOK = 1024
NCH = D // 128
L_PROG = [16, 14, 12, 10, 8, 6, 4, 2]
SLOTS_A = [15, 13, 11, 9, 7, 5, 3, 1]
SLOTS_B = [14, 12, 10, 8, 6, 4, 2, 0]

SC_A = float(np.float32(1.0) / np.float32(np.sqrt(np.float64(128.0))))
N_GU = 2048 * 11264
N_D = 5632 * 2048

ENGS = ("sync", "tensor", "vector", "scalar", "gpsimd")
DEBUG = False


class Ctr:
    _next = [0]

    def __init__(self, name):
        self.idx = Ctr._next[0]
        Ctr._next[0] += 1
        self.name = name
        self.v = 0


def build_program():
    Ctr._next[0] = 0
    nc = bass.Bass("TRN2", target_bir_lowering=False)

    din = lambda n, sh, dt=F32: nc.dram_tensor(n, sh, dt, kind="ExternalInput").ap()
    dsc = lambda n, sh, dt=F32: nc.dram_tensor(
        n, sh, dt, kind="ExternalOutput" if DEBUG else "Internal").ap()
    xT = din("xT", [D, S])
    xTq = din("xTq", [D, TOK])
    xown = din("xown", [TOK, D])
    wq = din("wq", [D, D]); wk = din("wk", [D, D])
    wv = din("wv", [D, D]); wo = din("wo", [D, D])
    wguT = din("wguT", [D, 2 * FF])
    wdT = din("wdT", [FF, D])
    nw_in = din("nw_in", [1, D])
    identb_in = din("identb_in", [128, 128], BF16)
    identf_in = din("identf_in", [128, 128], F32)
    masks_in = din("masks_in", [128, 8 * 256], F32)
    out = nc.dram_tensor("out", [TOK, D], F32, kind="ExternalOutput").ap()

    sKTh = dsc("sKTh", [D, S], BF16); sKTl = dsc("sKTl", [D, S], BF16)
    sVh = dsc("sVh", [S, D], BF16); sVl = dsc("sVl", [S, D], BF16)
    sQTh = dsc("sQTh", [D, TOK], BF16); sQTl = dsc("sQTl", [D, TOK], BF16)
    sX1 = dsc("sX1", [TOK, D]); sR1 = dsc("sR1", [TOK, D])
    sHU = dsc("sHU", [TOK, FF]); sR2 = dsc("sR2", [TOK, D])
    sSC = dsc("sSC", [1, 8])

    sb = lambda n, sh, dt=F32: nc.alloc_sbuf_tensor(n, sh, dt).ap()
    identb = sb("identb", [128, 128], BF16)
    identf = sb("identf", [128, 128], F32)
    masks = sb("masks", [128, 8 * 256], F32)
    nw = sb("nw", [128, D], F32)
    ones1 = sb("ones1", [128, 1], F32)
    ST = sb("ST", [128, 160], F32)
    ST2 = sb("ST2", [128, 64], F32)
    SCb = sb("SCb", [128, 8], F32)
    ABF = sb("ABF", [128, 57344], BF16)
    AFP = sb("AFP", [128, 16384], F32)
    PS = nc.alloc_psum_tensor("PS", [128, 4096], F32).ap()

    bf = lambda o, n: ABF[:, o:o + n]
    fp = lambda o, n: AFP[:, o:o + n]
    bank = lambda i: PS[:, 512 * i:512 * (i + 1)]

    events = {e: [] for e in ENGS}
    all_ctrs = []

    def C(name):
        c = Ctr(name)
        all_ctrs.append(c)
        return c

    cTE, cVE, cSC = C("TE"), C("VE"), C("SC")
    dCONST = C("dCONST")
    dMEAN = [C("dMEAN0"), C("dMEAN1"), C("dMEAN2")]
    dXQ = [C("dXQ0"), C("dXQ1")]
    dWQ = [C("dWQ0"), C("dWQ1")]
    dQS = [C("dQS0"), C("dQS1")]
    dKS = [C("dKS0"), C("dKS1")]
    dSC1, dSC2 = C("dSC1"), C("dSC2")
    dHD = [C("dHD0"), C("dHD1")]
    dWO = [C("dWO0"), C("dWO1")]
    dXO = [C("dXO0"), C("dXO1")]
    dRS = [C("dRS0"), C("dRS1")]
    dR1L = [C("dR1L0"), C("dR1L1")]
    dX1S = [C("dX1S0"), C("dX1S1")]
    dX1L = [C("dX1L0"), C("dX1L1")]
    dGU = [C("dGU0"), C("dGU1"), C("dGU2"), C("dGU3")]
    dHUS = [C("dHUS0"), C("dHUS1")]
    dHUL = [C("dHUL0"), C("dHUL1")]
    dTD = [C("dTD0"), C("dTD1")]
    dXSL = [C("dXSL0"), C("dXSL1")]
    dR2S = [C("dR2S0"), C("dR2S1")]
    dR2L = [C("dR2L0"), C("dR2L1")]
    dOUT = [C("dOUT0"), C("dOUT1")]

    def E(eng, fn, waits=(), inc=None, n=1):
        events[eng].append((tuple(waits), fn, (inc.idx if inc else None, n)))
        if inc is not None:
            inc.v += n
        return inc.v if inc is not None else None

    def dma(eng_name, dst, src, c, waits=()):
        return E(eng_name, lambda e, d=dst, s=src: e.dma_start(out=d, in_=s),
                 waits=waits, inc=c, n=16)

    W = lambda c, v=None: (c.idx, c.v if v is None else v)

    dma("sync", identb[:], identb_in, dCONST)
    dma("sync", identf[:], identf_in, dCONST)
    dma("sync", masks[:], masks_in, dCONST)
    dma("sync", nw[:], nw_in.broadcast_to([128, D]), dCONST)
    const_done = dCONST.v
    E("vector", lambda e: e.memset(ones1[:], 1.0), inc=cVE)
    E("vector", lambda e: e.memset(ST[:], 0.0), inc=cVE)
    st_zero = cVE.v

    mean_tiles = []
    col = 0
    for i in range(NCH):
        for j0 in range(0, 2 * FF, 2048):
            ncol = min(2048, 2 * FF - j0)
            mean_tiles.append((wguT[128 * i:128 * (i + 1), j0:j0 + ncol], ncol, col))
            col += 1
    n_gu_tiles = col
    for i in range(FF // 128):
        mean_tiles.append((wdT[128 * i:128 * (i + 1), :], 2048, col))
        col += 1
    n_mean_tiles = col

    mean_sc_free = [0, 0, 0]
    mean_idx = [0]

    def emit_mean(n):
        for _ in range(n):
            k = mean_idx[0]
            if k >= len(mean_tiles):
                return
            mean_idx[0] = k + 1
            msrc, ncol, c_ = mean_tiles[k]
            buf = fp(4096 + 2048 * (k % 3), ncol)
            scrap = fp(12288, ncol)
            ld = dma("gpsimd", buf, msrc, dMEAN[k % 3],
                     waits=[W(cSC, mean_sc_free[k % 3]), (cVE.idx, st_zero)])
            mean_sc_free[k % 3] = E(
                "scalar",
                lambda e, b=buf, cc=c_, s_=scrap: e.activation(
                    s_, b, AF.Abs, accum_out=ST[:, cc:cc + 1]),
                waits=[W(dMEAN[k % 3], ld)], inc=cSC)

    ld_free = [0, 0]
    for i in range(NCH):
        p = i % 2
        buf = fp(1024 + 1024 * p, 1024)
        ld = dma("sync", buf, xTq[128 * i:128 * (i + 1), :], dXQ[p],
                 waits=[W(cVE, ld_free[p])])
        hc = E("scalar", lambda e, b=buf, i_=i: e.activation(
            bf(1024 * i_, 1024), b, AF.Copy), waits=[W(dXQ[p], ld)], inc=cSC)
        ld_free[p] = E("vector",
                       lambda e, b=buf, i_=i: e.tensor_tensor(
                           bf(16384 + 1024 * i_, 1024), b, bf(1024 * i_, 1024),
                           op=OP.subtract), waits=[W(cSC, hc)], inc=cVE)
    xq_ready = cVE.v
    xq_ready_sc = cSC.v

    qev_free = [0, 0]
    ps_evac_done = 0
    wq_ld_free = [0, 0]
    prev_qmm = 0
    for g in range(4):
        w_ready = []
        for i in range(NCH):
            p = i % 2
            buf = fp(512 * p, 512)
            ld = dma("sync", buf, wq[128 * i:128 * (i + 1), 512 * g:512 * (g + 1)],
                     dWQ[p], waits=[W(cVE, wq_ld_free[p])])
            hc = E("scalar", lambda e, b=buf, i_=i: e.activation(
                bf(32768 + 512 * i_, 512), b, AF.Copy),
              waits=[W(dWQ[p], ld), W(cTE, prev_qmm)], inc=cSC)
            wq_ld_free[p] = E("vector",
                              lambda e, b=buf, i_=i: e.tensor_tensor(
                                  bf(40960 + 512 * i_, 512), b,
                                  bf(32768 + 512 * i_, 512), op=OP.subtract),
                              waits=[W(cSC, hc)], inc=cVE)
            w_ready.append(cVE.v)

        def mk_mmq(i):
            def mmq(e, i_=i):
                wh = bf(32768 + 512 * i_, 512)
                wl = bf(40960 + 512 * i_, 512)
                for jb in range(4):
                    lh = wh[:, 128 * jb:128 * (jb + 1)]
                    ll = wl[:, 128 * jb:128 * (jb + 1)]
                    for t2 in range(2):
                        xh = bf(1024 * i_ + 512 * t2, 512)
                        xl = bf(16384 + 1024 * i_ + 512 * t2, 512)
                        bk = PS[:, 1024 * jb + 512 * t2: 1024 * jb + 512 * t2 + 512]
                        e.matmul(bk, lh, xh, start=(i_ == 0), stop=False)
                        e.matmul(bk, lh, xl, start=False, stop=False)
                        mm = e.matmul(bk, ll, xh, start=False, stop=(i_ == NCH - 1))
                return mm
            return mmq
        for i in range(NCH):
            wts = [W(cVE, w_ready[i])]
            if i == 0:
                wts += [W(cVE, xq_ready), W(cVE, ps_evac_done)]
            q_mm = E("tensor", mk_mmq(i), waits=wts, inc=cTE)
        prev_qmm = q_mm
        for jb in range(4):
            p = jb % 2
            pcol = PS[:, 1024 * jb:1024 * (jb + 1)]
            t2 = fp(11264, 1024)
            qh = bf(49152 + 2048 * p, 1024)
            ql = bf(49152 + 2048 * p + 1024, 1024)
            E("vector", lambda e, pc=pcol, tb=t2: e.tensor_scalar(
                tb, pc, SC_A, None, op0=OP.mult),
              waits=[W(cTE, q_mm), W(dQS[p], qev_free[p])], inc=cVE)
            E("vector", lambda e, tb=t2, q=qh: e.tensor_copy(q, tb), inc=cVE)
            ev = E("vector", lambda e, tb=t2, q=qh, l=ql: e.tensor_tensor(
                l, tb, q, op=OP.subtract), inc=cVE)
            if jb >= 2:
                ps_evac_done = ev
            r0 = 512 * g + 128 * jb
            dma("sync", sQTh[r0:r0 + 128, :], qh, dQS[p], waits=[W(cVE, ev)])
            qev_free[p] = dma("sync", sQTl[r0:r0 + 128, :], ql, dQS[p])
        emit_mean(7)
    p1a_qs_final = (dQS[0].v, dQS[1].v)

    ev_first = [True, True]
    ev_free = [0, 0]
    w_ld_free = [max(wq_ld_free), max(wq_ld_free)]
    kv_ps_free = ps_evac_done
    prev_kvmm = prev_qmm
    for th in range(2):
        for i in range(NCH):
            p = i % 2
            buf = fp(1024 + 1024 * p, 1024)
            ld = dma("sync", buf, xT[128 * i:128 * (i + 1), 1024 * th:1024 * (th + 1)],
                     dXQ[p], waits=[W(cVE, ld_free[p])])
            hc = E("scalar", lambda e, b=buf, i_=i: e.activation(
                bf(1024 * i_, 1024), b, AF.Copy), waits=[W(dXQ[p], ld)], inc=cSC)
            ld_free[p] = E("vector",
                           lambda e, b=buf, i_=i: e.tensor_tensor(
                               bf(16384 + 1024 * i_, 1024), b, bf(1024 * i_, 1024),
                               op=OP.subtract), waits=[W(cSC, hc)], inc=cVE)
        x_ready = cVE.v

        for g in range(4):
            w_ready = []
            for i in range(NCH):
                p = i % 2
                buf = fp(512 * p, 512)
                ld = dma("sync", buf, wk[128 * i:128 * (i + 1), 512 * g:512 * (g + 1)],
                         dWQ[p], waits=[W(cVE, w_ld_free[p])])
                hc = E("scalar", lambda e, b=buf, i_=i: e.activation(
                    bf(32768 + 512 * i_, 512), b, AF.Copy),
                  waits=[W(dWQ[p], ld), W(cTE, prev_kvmm)], inc=cSC)
                w_ld_free[p] = E("vector",
                                 lambda e, b=buf, i_=i: e.tensor_tensor(
                                     bf(40960 + 512 * i_, 512), b,
                                     bf(32768 + 512 * i_, 512), op=OP.subtract),
                                 waits=[W(cSC, hc)], inc=cVE)
                w_ready.append(cVE.v)

            def mk_mmk(i):
                def mmk(e, i_=i):
                    for jb in range(4):
                        lh = bf(32768 + 512 * i_, 512)[:, 128 * jb:128 * (jb + 1)]
                        ll = bf(40960 + 512 * i_, 512)[:, 128 * jb:128 * (jb + 1)]
                        for t2 in range(2):
                            xh = bf(1024 * i_ + 512 * t2, 512)
                            xl = bf(16384 + 1024 * i_ + 512 * t2, 512)
                            bk = PS[:, 1024 * jb + 512 * t2:1024 * jb + 512 * t2 + 512]
                            e.matmul(bk, lh, xh, start=(i_ == 0), stop=False)
                            e.matmul(bk, lh, xl, start=False, stop=False)
                            mm = e.matmul(bk, ll, xh, start=False, stop=(i_ == NCH - 1))
                    return mm
                return mmk
            for i in range(NCH):
                wts = [W(cVE, w_ready[i])]
                if i == 0:
                    wts += [W(cVE, x_ready), W(cVE, kv_ps_free)]
                k_mm = E("tensor", mk_mmk(i), waits=wts, inc=cTE)
            prev_kvmm = k_mm
            for jb in range(4):
                p = jb % 2
                pcol = PS[:, 1024 * jb:1024 * (jb + 1)]
                kh = bf(49152 + 2048 * p, 1024)
                kl = bf(49152 + 2048 * p + 1024, 1024)
                evw = [W(cTE, k_mm)]
                if ev_first[p]:
                    evw.append(W(dQS[p], p1a_qs_final[p]))
                    ev_first[p] = False
                else:
                    evw.append(W(dKS[p], ev_free[p]))
                hc = E("scalar", lambda e, pc=pcol, q=kh: e.activation(q, pc, AF.Copy),
                  waits=evw, inc=cSC)
                ev = E("vector", lambda e, pc=pcol, q=kh, l=kl: e.tensor_tensor(
                    l, pc, q, op=OP.subtract), waits=[W(cSC, hc)], inc=cVE)
                if jb >= 2:
                    kv_ps_free = ev
                r0 = 512 * g + 128 * jb
                dma("sync", sKTh[r0:r0 + 128, 1024 * th:1024 * (th + 1)], kh,
                    dKS[p], waits=[W(cVE, ev)])
                ev_free[p] = dma("sync", sKTl[r0:r0 + 128, 1024 * th:1024 * (th + 1)],
                                 kl, dKS[p])
            emit_mean(7)

        for jq in range(4):
            w_ready = []
            for i in range(NCH):
                p = i % 2
                buf = fp(512 * p, 512)
                ld = dma("sync", buf, wv[128 * i:128 * (i + 1), 512 * jq:512 * (jq + 1)],
                         dWQ[p], waits=[W(cVE, w_ld_free[p])])
                hc = E("scalar", lambda e, b=buf, i_=i: e.activation(
                    bf(32768 + 512 * i_, 512), b, AF.Copy),
                  waits=[W(dWQ[p], ld), W(cTE, prev_kvmm)], inc=cSC)
                w_ld_free[p] = E("vector",
                                 lambda e, b=buf, i_=i: e.tensor_tensor(
                                     bf(40960 + 512 * i_, 512), b,
                                     bf(32768 + 512 * i_, 512), op=OP.subtract),
                                 waits=[W(cSC, hc)], inc=cVE)
                w_ready.append(cVE.v)

            def mk_mmv(i):
                def mmv(e, i_=i):
                    for tb in range(8):
                        lh = bf(1024 * i_, 1024)[:, 128 * tb:128 * (tb + 1)]
                        ll = bf(16384 + 1024 * i_, 1024)[:, 128 * tb:128 * (tb + 1)]
                        wh = bf(32768 + 512 * i_, 512)
                        wl = bf(40960 + 512 * i_, 512)
                        bk = bank(tb)
                        e.matmul(bk, lh, wh, start=(i_ == 0), stop=False)
                        e.matmul(bk, lh, wl, start=False, stop=False)
                        mm = e.matmul(bk, ll, wh, start=False, stop=(i_ == NCH - 1))
                    return mm
                return mmv
            for i in range(NCH):
                wts = [W(cVE, w_ready[i])]
                if i == 0:
                    wts += [W(cVE, x_ready), W(cVE, kv_ps_free)]
                v_mm = E("tensor", mk_mmv(i), waits=wts, inc=cTE)
            prev_kvmm = v_mm
            for tb in range(8):
                p = tb % 2
                vh = bf(49152 + 2048 * p, 512)
                vl = bf(49152 + 2048 * p + 512, 512)
                hc = E("scalar", lambda e, t=tb, q=vh: e.activation(q, bank(t), AF.Copy),
                  waits=[W(cTE, v_mm), W(dKS[p], ev_free[p])], inc=cSC)
                ev = E("vector", lambda e, t=tb, q=vh, l=vl: e.tensor_tensor(
                    l, bank(t), q, op=OP.subtract), waits=[W(cSC, hc)], inc=cVE)
                if tb >= 6:
                    kv_ps_free = ev
                r0 = 1024 * th + 128 * tb
                dma("sync", sVh[r0:r0 + 128, 512 * jq:512 * (jq + 1)], vh,
                    dKS[p], waits=[W(cVE, ev)])
                ev_free[p] = dma("sync", sVl[r0:r0 + 128, 512 * jq:512 * (jq + 1)],
                                 vl, dKS[p])
            emit_mean(7)
    p1_ks_final = (dKS[0].v, dKS[1].v)
    emit_mean(len(mean_tiles))
    assert mean_idx[0] == len(mean_tiles)
    mean_acts_done = cSC.v

    def mean_stats(e):
        e.reduce_sum(ST2[:, 0:1], ST[:, 0:n_gu_tiles], axis=AX)
        e.reduce_sum(ST2[:, 1:2], ST[:, n_gu_tiles:n_mean_tiles], axis=AX)
        return e.drain()
    ms1 = E("vector", mean_stats, waits=[W(cSC, mean_acts_done)], inc=cVE)
    msum = E("tensor", lambda e: e.matmul(PS[0:1, 0:2], ones1[:], ST2[:, 0:2],
                                          start=True, stop=True),
             waits=[W(cVE, ms1), W(cVE, kv_ps_free)], inc=cTE)

    def mean_math(e):
        r = ST2[0:1, :]
        e.tensor_copy(r[:, 4:6], PS[0:1, 0:2])
        e.drain()
        e.memset(r[:, 6:7], float(N_GU))
        e.memset(r[:, 7:8], float(N_D))
        e.drain()
        e.reciprocal(r[:, 8:9], r[:, 6:7])
        e.reciprocal(r[:, 9:10], r[:, 7:8])
        e.drain()
        e.tensor_tensor(r[:, 10:11], r[:, 4:5], r[:, 8:9], op=OP.mult)
        e.tensor_tensor(r[:, 11:12], r[:, 5:6], r[:, 9:10], op=OP.mult)
        e.drain()
        e.tensor_scalar(r[:, 12:13], r[:, 10:11], 1e-5, None, op0=OP.max)
        e.tensor_scalar(r[:, 13:14], r[:, 11:12], 1e-5, None, op0=OP.max)
        e.drain()
        e.tensor_scalar(r[:, 16:17], r[:, 12:13], 0.5, None, op0=OP.mult)
        e.tensor_scalar(r[:, 20:21], r[:, 13:14], 0.5, None, op0=OP.mult)
        e.reciprocal(r[:, 15:16], r[:, 12:13])
        e.tensor_scalar(r[:, 19:20], r[:, 13:14], 0.5, None, op0=OP.mult)
        e.drain()
        e.tensor_scalar(r[:, 18:19], r[:, 15:16], 2.0, None, op0=OP.mult)
        e.tensor_scalar(r[:, 17:18], r[:, 16:17], -1.0, None, op0=OP.mult)
        e.tensor_scalar(r[:, 21:22], r[:, 20:21], -1.0, None, op0=OP.mult)
        return e.drain()
    ms2 = E("vector", mean_math, waits=[W(cTE, msum)], inc=cVE)
    dma("sync", sSC[0:1, 0:6], ST2[0:1, 16:22], dSC1, waits=[W(cVE, ms2)])
    dma("sync", SCb[:, 0:6], sSC[0:1, 0:6].broadcast_to([128, 6]), dSC2,
        waits=[W(dSC1, 16)])
    scb_done = dSC2.v

    hb_free = [0, 0]
    s_banks_free = 0
    m_bank_free = 0
    pt_free = [0, 0]
    av_free2 = [0]
    p_free = ms2
    first_smm = [True]
    first_odd_ps = [True]

    for h in range(NH):
        p = h % 2
        base = 32768 + 10240 * p
        kth = bf(base, 2048); ktl = bf(base + 2048, 2048)
        vh = bf(base + 4096, 2048); vl = bf(base + 6144, 2048)
        qth = bf(base + 8192, 1024); qtl = bf(base + 9216, 1024)
        hw_waits = [W(cTE, hb_free[p]),
                    W(dQS[0], p1a_qs_final[0]), W(dQS[1], p1a_qs_final[1]),
                    W(dKS[0], p1_ks_final[0]), W(dKS[1], p1_ks_final[1])]
        dma("sync", kth, sKTh[128 * h:128 * (h + 1), :], dHD[p], waits=hw_waits)
        dma("sync", ktl, sKTl[128 * h:128 * (h + 1), :], dHD[p])
        vsrc_h = sVh.rearrange("(tb p) j -> p tb j", p=128)[:, :, 128 * h:128 * (h + 1)]
        vsrc_l = sVl.rearrange("(tb p) j -> p tb j", p=128)[:, :, 128 * h:128 * (h + 1)]
        dma("sync", vh.rearrange("p (tb d) -> p tb d", d=128), vsrc_h, dHD[p])
        dma("sync", vl.rearrange("p (tb d) -> p tb d", d=128), vsrc_l, dHD[p])
        dma("sync", qth, sQTh[128 * h:128 * (h + 1), :], dHD[p])
        hload = dma("sync", qtl, sQTl[128 * h:128 * (h + 1), :], dHD[p])

        pending = None

        def back(pend):
            g_, L_, ps_v, vh_, vl_, hb_slot, pb_ = pend
            for kj in range(L_):
                par = kj % 2
                tr = E("tensor", lambda e, k=kj, par_=par, pb=pb_: e.transpose(
                    PS[:, 2560 + 512 * par_:2560 + 512 * par_ + 128],
                    fp(pb + 128 * k, 128), identf[:]),
                    waits=[W(cVE, ps_v), W(cVE, pt_free[par])], inc=cTE)
                hc = E("scalar", lambda e, k=kj, par_=par: e.activation(
                    bf(53248 + 128 * k, 128),
                    PS[:, 2560 + 512 * par_:2560 + 512 * par_ + 128], AF.Copy),
                  waits=[W(cTE, tr)], inc=cSC)
                pt_free[par] = E("vector", lambda e, k=kj, par_=par: e.tensor_tensor(
                    bf(55296 + 128 * k, 128),
                    PS[:, 2560 + 512 * par_:2560 + 512 * par_ + 128],
                    bf(53248 + 128 * k, 128), op=OP.subtract),
                    waits=[W(cSC, hc)], inc=cVE)
            pt_done = cVE.v

            def mpv(e, L__=L_, vh__=vh_, vl__=vl_):
                for kj in range(L__):
                    lv_h = vh__[:, 128 * kj:128 * (kj + 1)]
                    lv_l = vl__[:, 128 * kj:128 * (kj + 1)]
                    rh = bf(53248 + 128 * kj, 128)
                    rl = bf(55296 + 128 * kj, 128)
                    bk = PS[:, 3584:3712]
                    e.matmul(bk, lv_h, rh, start=(kj == 0), stop=False)
                    e.matmul(bk, lv_h, rl, start=False, stop=False)
                    mm = e.matmul(bk, lv_l, rh, start=False, stop=(kj == L__ - 1))
                return mm
            pv_mm = E("tensor", mpv, waits=[W(cVE, pt_done), W(cVE, av_free2[0])],
                      inc=cTE)
            if hb_slot is not None:
                hb_free[hb_slot] = pv_mm
            hc = E("scalar", lambda e, h_=h, g__=g_: e.activation(
                bf(1024 * h_ + 128 * g__, 128), PS[:, 3584:3712], AF.Copy),
              waits=[W(cTE, pv_mm)], inc=cSC)
            av_free2[0] = E("vector", lambda e, h_=h, g__=g_: e.tensor_tensor(
                bf(16384 + 1024 * h_ + 128 * g__, 128), PS[:, 3584:3712],
                bf(1024 * h_ + 128 * g__, 128), op=OP.subtract),
                waits=[W(cSC, hc)], inc=cVE)

        for g in range(8):
            L = L_PROG[g]
            W_ = L * 128
            qh_ = qth[:, 128 * g:128 * (g + 1)]
            ql_ = qtl[:, 128 * g:128 * (g + 1)]

            def mms(e, L_=L, qh=qh_, ql=ql_, kth_=kth, ktl_=ktl):
                n_left = L_ * 128
                o = 0
                while n_left > 0:
                    n = min(512, n_left)
                    bk = PS[:, o:o + n]
                    e.matmul(bk, qh, kth_[:, o:o + n], start=True, stop=False)
                    e.matmul(bk, qh, ktl_[:, o:o + n], start=False, stop=False)
                    mm = e.matmul(bk, ql, kth_[:, o:o + n], start=False, stop=True)
                    o += n; n_left -= n
                return mm
            smw = [W(dHD[p], hload), W(cSC, s_banks_free)]
            if first_smm[0]:
                smw.append(W(cVE, p_free))
                first_smm[0] = False
            s_mm = E("tensor", mms, waits=smw, inc=cTE)
            mk = E("vector", lambda e, L_=L, g_=g: e.tensor_tensor(
                PS[:, 2048:2304], PS[:, (L_ - 2) * 128:L_ * 128],
                masks[:, 256 * g_:256 * (g_ + 1)], op=OP.add),
                waits=[W(cTE, s_mm), W(cSC, m_bank_free), W(dCONST, const_done)],
                inc=cVE)
            c0, c1 = 24, 25
            if L > 2:
                E("scalar", lambda e, L_=L: e.activation(
                    fp(0, (L_ - 2) * 128), PS[:, 0:(L_ - 2) * 128], AF.Exp,
                    accum_out=ST2[:, c0:c0 + 1]),
                  waits=[W(cVE, mk), W(cVE, p_free)], inc=cSC)
                s_banks_free = cSC.v
            m_bank_free = E("scalar", lambda e, L_=L: e.activation(
                fp((L_ - 2) * 128, 256), PS[:, 2048:2304], AF.Exp,
                accum_out=ST2[:, c1:c1 + 1]),
                waits=[W(cVE, mk)] + ([] if L > 2 else [W(cVE, p_free)]), inc=cSC)
            if L == 2:
                s_banks_free = m_bank_free
            exp_done = cSC.v

            def den_rcp(e, L_=L):
                if L_ > 2:
                    e.tensor_tensor(ST2[:, 26:27], ST2[:, c0:c0 + 1],
                                    ST2[:, c1:c1 + 1], op=OP.add)
                else:
                    e.tensor_copy(ST2[:, 26:27], ST2[:, c1:c1 + 1])
                e.drain()
                e.reciprocal(ST2[:, 27:28], ST2[:, 26:27])
                return e.drain()
            E("vector", den_rcp, waits=[W(cSC, exp_done)], inc=cVE)
            psbase = 2048 + 2048 * (g % 2)
            psw = [W(cSC, mean_acts_done)] if first_odd_ps[0] and g % 2 == 1 else []
            if g % 2 == 1:
                first_odd_ps[0] = False
            ps_ = E("vector", lambda e, W__=W_, pb=psbase: e.tensor_scalar(
                fp(pb, W__), fp(0, W__), ST2[:, 27:28], None, op0=OP.mult),
                waits=psw, inc=cVE)
            p_free = ps_

            if pending is not None:
                back(pending)
            pending = (g, L, ps_, vh, vl, p if g == 7 else None, psbase)
        back(pending)
    attnT_done = cVE.v
    attnT_done = cVE.v

    wo_free = [attnT_done, attnT_done]
    xo_free = [attnT_done, attnT_done]
    rs_free = [0, 0]
    o_ps_free = av_free2[0]
    for jt in range(4):
        mm_prev = 0
        for i in range(NCH):
            p = i % 2
            buf = fp(512 * p, 512)
            ld = dma("sync", buf, wo[128 * i:128 * (i + 1), 512 * jt:512 * (jt + 1)],
                     dWO[p], waits=[W(cVE, wo_free[p])])
            hc = E("scalar", lambda e, b=buf, p_=p: e.activation(
                bf(32768 + 512 * p_, 512), b, AF.Copy),
              waits=[W(dWO[p], ld), W(cTE, mm_prev)], inc=cSC)
            wo_free[p] = E("vector", lambda e, b=buf, p_=p: e.tensor_tensor(
                bf(33792 + 512 * p_, 512), b, bf(32768 + 512 * p_, 512),
                op=OP.subtract), waits=[W(cSC, hc)], inc=cVE)

            def mmo(e, i_=i, p_=p):
                wh = bf(32768 + 512 * p_, 512)
                wl = bf(33792 + 512 * p_, 512)
                for tb in range(8):
                    ah = bf(1024 * i_ + 128 * tb, 128)
                    al = bf(16384 + 1024 * i_ + 128 * tb, 128)
                    bk = bank(tb)
                    e.matmul(bk, ah, wh, start=(i_ == 0), stop=False)
                    e.matmul(bk, ah, wl, start=False, stop=False)
                    mm = e.matmul(bk, al, wh, start=False, stop=(i_ == NCH - 1))
                return mm
            mm_prev = E("tensor", mmo,
                        waits=[W(cVE, wo_free[p]), W(cVE, attnT_done),
                               W(cVE, o_ps_free)], inc=cTE)
        for tb in range(8):
            p = tb % 2
            xo = fp(1024 + 512 * p, 512)
            ld = dma("sync", xo, xown[128 * tb:128 * (tb + 1), 512 * jt:512 * (jt + 1)],
                     dXO[p], waits=[W(cVE, xo_free[p])])
            rt = fp(5120 + 512 * p, 512)
            xo_free[p] = E("vector", lambda e, t=tb, x=xo, r=rt: e.tensor_tensor(
                r, bank(t), x, op=OP.add),
                waits=[W(cTE, mm_prev), W(dXO[p], ld), W(dRS[p], rs_free[p])],
                inc=cVE)
            if tb >= 6:
                o_ps_free = cVE.v
            rs_free[p] = dma("sync", sR1[128 * tb:128 * (tb + 1), 512 * jt:512 * (jt + 1)],
                             rt, dRS[p], waits=[W(cVE, xo_free[p])])
    p3_rs_final = (dRS[0].v, dRS[1].v)
    p3_last_mm = mm_prev

    rt_free = [0, 0]
    x1_free = [0, 0]
    for tb in range(8):
        p = tb % 2
        rt = fp(6144 + 2048 * p, 2048)
        ld = dma("sync", rt, sR1[128 * tb:128 * (tb + 1), :], dR1L[p],
                 waits=[W(dRS[0], p3_rs_final[0]), W(dRS[1], p3_rs_final[1]),
                        W(cVE, rt_free[p])])

        def ln_stats(e, r=rt):
            e.reduce_sum(ST2[:, 28:29], r, axis=AX)
            e.drain()
            e.tensor_scalar(ST2[:, 29:30], ST2[:, 28:29], -(2.0 ** -11), None,
                            op0=OP.mult)
            return e.drain()
        lm = E("vector", ln_stats, waits=[W(dR1L[p], ld)], inc=cVE)
        sq = E("scalar", lambda e, r=rt: e.activation(
            fp(2048, 2048), r, AF.Square, bias=ST2[:, 29:30],
            accum_out=ST2[:, 30:31]), waits=[W(cVE, lm)], inc=cSC)

        def ln_rstd1(e):
            e.tensor_scalar(ST2[:, 31:32], ST2[:, 30:31], 2.0 ** -11, 1e-5,
                            op0=OP.mult, op1=OP.add)
            return e.drain()
        v1_ = E("vector", ln_rstd1, waits=[W(cSC, sq)], inc=cVE)
        sq2 = E("scalar", lambda e: e.activation(ST2[:, 32:33], ST2[:, 31:32],
                                                 AF.Sqrt), waits=[W(cVE, v1_)],
                inc=cSC)

        def ln_fin(e, r=rt, p_=p):
            e.reciprocal(ST2[:, 33:34], ST2[:, 32:33])
            e.drain()
            x1t = fp(10240 + 2048 * p_, 2048)
            e.tensor_scalar(fp(14336, 2048), r, ST2[:, 29:30], None, op0=OP.add)
            e.drain()
            return e.tensor_scalar(x1t, fp(14336, 2048), ST2[:, 33:34], None,
                                   op0=OP.mult)
        fin = E("vector", ln_fin, waits=[W(cSC, sq2), W(dX1S[p], x1_free[p])],
                inc=cVE)
        rt_free[p] = fin
        x1_free[p] = dma("sync", sX1[128 * tb:128 * (tb + 1), :],
                         fp(10240 + 2048 * p, 2048), dX1S[p], waits=[W(cVE, fin)])
    p3_x1s_final = (dX1S[0].v, dX1S[1].v)
    ln1_last_fin = rt_free[1]

    x1t_free = [0, 0]
    pt4_free = [0, 0]
    prev_ladder = 0
    gamcol = 8
    for tb in range(8):
        p = tb % 2
        x1t = fp(2048 * p, 2048)
        ld = dma("sync", x1t, sX1[128 * tb:128 * (tb + 1), :], dX1L[p],
                 waits=[W(dX1S[0], p3_x1s_final[0]), W(dX1S[1], p3_x1s_final[1]),
                        W(cVE, x1t_free[p])])
        sq = E("scalar", lambda e, x=x1t: e.activation(
            fp(12288, 2048), x, AF.Square, accum_out=ST2[:, 34:35]),
            waits=[W(dX1L[p], ld), W(cVE, prev_ladder)], inc=cSC)

        def rms1(e):
            e.tensor_scalar(ST2[:, 35:36], ST2[:, 34:35], 2.0 ** -11, 1e-5,
                            op0=OP.mult, op1=OP.add)
            return e.drain()
        v1_ = E("vector", rms1, waits=[W(cSC, sq)], inc=cVE)
        s2 = E("scalar", lambda e: e.activation(ST2[:, 36:37], ST2[:, 35:36],
                                                AF.Sqrt), waits=[W(cVE, v1_)],
               inc=cSC)

        def rms2(e, x=x1t, t=tb):
            e.reciprocal(ST2[:, 37:38], ST2[:, 36:37])
            e.drain()
            e.tensor_scalar(fp(6144, 2048), x, ST2[:, 37:38], None, op0=OP.mult)
            e.drain()
            e.tensor_tensor(fp(4096, 2048), fp(6144, 2048), nw[:], op=OP.mult)
            e.drain()
            e.reduce_max(ST2[:, 38:39], fp(4096, 2048), axis=AX,
                         apply_absolute_value=True)
            e.drain()
            e.tensor_scalar(ST2[:, 39:40], ST2[:, 38:39], 1e-5, None, op0=OP.max)
            e.drain()
            e.reciprocal(ST2[:, 40:41], ST2[:, 39:40])
            e.drain()
            e.tensor_scalar(ST2[:, gamcol + t:gamcol + t + 1], ST2[:, 40:41],
                            127.0, None, op0=OP.mult)
            return e.drain()
        x1t_free[p] = E("vector", rms2, waits=[W(cSC, s2), W(dCONST, const_done)],
                        inc=cVE)

        def ladder(e, t=tb):
            g_ = ST2[:, gamcol + t:gamcol + t + 1]
            e.tensor_scalar(fp(6144, 2048), fp(4096, 2048), g_, 128.5,
                            op0=OP.mult, op1=OP.add)
            bufs = [fp(6144, 2048), fp(10240, 2048), fp(12288, 2048)]
            cur = 0
            for b_ in range(7, -1, -1):
                s_ = bufs[cur]
                d_ = bufs[1] if cur != 1 else bufs[2]
                e.drain()
                e.tensor_scalar(fp(8192, 2048), s_, float(2 ** b_),
                                float(2 ** b_), op0=OP.is_ge, op1=OP.mult)
                e.drain()
                e.tensor_tensor(d_, s_, fp(8192, 2048), op=OP.subtract)
                cur = 1 if cur != 1 else 2
            rb_ = bufs[cur]
            e.drain()
            return e.scalar_tensor_tensor(fp(8192, 2048), fp(6144, 2048),
                                          128.0, rb_, op0=OP.subtract,
                                          op1=OP.subtract)
        lq = E("vector", ladder, inc=cVE)
        prev_ladder = lq
        for i in range(NCH):
            par = i % 2
            tr = E("tensor", lambda e, i_=i, par_=par: e.transpose(
                PS[:, 512 * par_:512 * par_ + 128],
                fp(8192 + 128 * i_, 128), identf[:]),
                waits=[W(cVE, lq), W(cVE, pt4_free[par]),
                       W(cVE, o_ps_free), W(cSC, s_banks_free)], inc=cTE)
            pt4_free[par] = E("vector", lambda e, i_=i, t=tb, par_=par: e.tensor_copy(
                bf(1024 * i_ + 128 * t, 128), PS[:, 512 * par_:512 * par_ + 128]),
                waits=[W(cTE, tr)], inc=cVE)
    xq_done = cVE.v

    gu_ld_free = [W(cVE, ln1_last_fin)] * 4
    sg_scrap_free = [W(cTE, p3_last_mm)] * 4
    hu_free = [0, 0]
    gu_ps_free = 0
    gu_ps_free_sc = 0
    st_reuse_ok = ms2
    for op_ in range(11):
        tq_ready = []
        for i in range(NCH):
            pg = i % 2
            bufg = fp(14336 + 512 * pg, 512)
            ldg = dma("sync", bufg,
                      wguT[128 * i:128 * (i + 1), 512 * op_:512 * (op_ + 1)],
                      dGU[pg], waits=[gu_ld_free[pg], W(dSC2, scb_done)])

            def mk_sg(e, b=bufg, sid=pg):
                e.activation(bf(32768 + 1024 * sid, 512), b, AF.Sign,
                             bias=SCb[:, 1:2])
                return e.activation(bf(32768 + 1024 * sid + 512, 512), b,
                                    AF.Sign, bias=SCb[:, 0:1])
            sg = E("scalar", mk_sg,
                   waits=[W(dGU[pg], ldg), sg_scrap_free[pg]], inc=cSC)
            gu_ld_free[pg] = W(cSC, sg)
            tg = E("vector", lambda e, sid=pg, i_=i: e.tensor_tensor(
                bf(16384 + 512 * i_, 512), bf(32768 + 1024 * sid, 512),
                bf(32768 + 1024 * sid + 512, 512), op=OP.add),
                waits=[W(cSC, sg)], inc=cVE)
            sg_scrap_free[pg] = W(cVE, tg)
            pu = 2 + (i % 2)
            bufu = fp(14336 + 512 * pu, 512)
            ldu = dma("sync", bufu,
                      wguT[128 * i:128 * (i + 1), FF + 512 * op_:FF + 512 * (op_ + 1)],
                      dGU[pu], waits=[gu_ld_free[pu]])

            def mk_su(e, b=bufu, sid=pu):
                e.activation(bf(32768 + 1024 * sid, 512), b, AF.Sign,
                             bias=SCb[:, 1:2])
                return e.activation(bf(32768 + 1024 * sid + 512, 512), b,
                                    AF.Sign, bias=SCb[:, 0:1])
            su = E("scalar", mk_su,
                   waits=[W(dGU[pu], ldu), sg_scrap_free[pu]], inc=cSC)
            gu_ld_free[pu] = W(cSC, su)
            tu = E("vector", lambda e, sid=pu, i_=i: e.tensor_tensor(
                bf(24576 + 512 * i_, 512), bf(32768 + 1024 * sid, 512),
                bf(32768 + 1024 * sid + 512, 512), op=OP.add),
                waits=[W(cSC, su)], inc=cVE)
            sg_scrap_free[pu] = W(cVE, tu)
            tq_ready.append(cVE.v)
        for tbh in range(2):
            def mmgu(e, tbh_=tbh):
                for i in range(NCH):
                    for t4 in range(4):
                        tb = 4 * tbh_ + t4
                        lh = bf(1024 * i + 128 * tb, 128)
                        e.matmul(bank(t4), lh, bf(16384 + 512 * i, 512),
                                 start=(i == 0), stop=(i == NCH - 1))
                        mm = e.matmul(bank(4 + t4), lh, bf(24576 + 512 * i, 512),
                                      start=(i == 0), stop=(i == NCH - 1))
                return mm
            gu_mm = E("tensor", mmgu,
                      waits=[W(cVE, tq_ready[-1]), W(cVE, gu_ps_free),
                             W(cSC, gu_ps_free_sc)] +
                            ([W(cVE, xq_done)] if op_ == 0 else []), inc=cTE)
            rs_scrap_free = [gu_ps_free, gu_ps_free]
            for t4 in range(4):
                tb = 4 * tbh + t4
                p = t4 % 2
                hu = fp(8192 + 512 * p, 512)
                rl = fp(6144 + 512 * p, 512)
                sq_ = fp(7168 + 512 * p, 512)

                def mk_rs(e, t4_=t4, rl_=rl, s_=sq_):
                    e.activation(rl_, bank(t4_), AF.Relu)
                    e.drain()
                    return e.activation(s_, rl_, AF.Square)
                rs = E("scalar", mk_rs,
                       waits=[W(cTE, gu_mm), W(cVE, rs_scrap_free[p])], inc=cSC)
                if t4 >= 2:
                    gu_ps_free_sc = rs

                def mk_hu(e, t4_=t4, h_=hu, s_=sq_, col=tb * 11 + op_):
                    e.tensor_tensor(h_, s_, bank(4 + t4_), op=OP.mult)
                    e.drain()
                    return e.reduce_max(ST[:, col:col + 1], h_, axis=AX,
                                        apply_absolute_value=True)
                hv = E("vector", mk_hu,
                       waits=[W(cSC, rs), W(dHUS[p], hu_free[p]),
                              W(cVE, st_reuse_ok)], inc=cVE)
                rs_scrap_free[p] = hv
                if t4 >= 2:
                    gu_ps_free = hv
                hu_free[p] = dma("sync", sHU[128 * tb:128 * (tb + 1),
                                             512 * op_:512 * (op_ + 1)],
                                 hu, dHUS[p], waits=[W(cVE, hv)])
    p4a_hus_final = (dHUS[0].v, dHUS[1].v)

    def hstats(e):
        for tb in range(8):
            e.drain()
            e.reduce_max(ST2[:, 41:42], ST[:, 11 * tb:11 * tb + 11], axis=AX)
            e.drain()
            e.tensor_copy(ST2[:, 42:43], ST2[:, 41:42])
            e.drain()
            e.tensor_scalar(ST2[:, 43:44], ST2[:, gamcol + tb:gamcol + tb + 1],
                            SCb[:, 2:3], None, op0=OP.mult)
            e.drain()
            e.reciprocal(ST2[:, 44:45], ST2[:, 43:44])
            e.drain()
            e.tensor_tensor(ST2[:, 45:46], ST2[:, 44:45], ST2[:, 44:45], op=OP.mult)
            e.drain()
            e.tensor_tensor(ST2[:, 46:47], ST2[:, 45:46], ST2[:, 44:45], op=OP.mult)
            e.drain()
            e.tensor_tensor(ST2[:, 47:48], ST2[:, 46:47], ST2[:, 42:43], op=OP.mult)
            e.drain()
            e.tensor_scalar(ST2[:, 41:42], ST2[:, 47:48], 1e-5, None, op0=OP.max)
            e.drain()
            e.reciprocal(ST2[:, 47:48], ST2[:, 41:42])
            e.drain()
            e.tensor_tensor(ST2[:, 44:45], ST2[:, 47:48], ST2[:, 46:47], op=OP.mult)
            e.drain()
            e.tensor_scalar(ST2[:, 48 + tb:49 + tb], ST2[:, 44:45], 127.0, None,
                            op0=OP.mult)
            e.drain()
            e.tensor_scalar(ST2[:, 56 + tb:57 + tb], ST2[:, 41:42], SCb[:, 3:4],
                            float(1.0 / 127.0), op0=OP.mult, op1=OP.mult)
        return e.drain()
    hst = E("vector", hstats,
            waits=[W(dHUS[0], p4a_hus_final[0]), W(dHUS[1], p4a_hus_final[1]),
                   W(dSC2, scb_done)], inc=cVE)

    hu_ld_free = [hst, hst]
    pt4_free = [cVE.v, cVE.v]
    segs = [(0, 2048), (2048, 2048), (4096, 1536)]
    for tb in range(8):
        for s0, sn in segs:
            p = (s0 // 2048) % 2
            hub = fp(2048 * p, sn)
            ld = dma("sync", hub, sHU[128 * tb:128 * (tb + 1), s0:s0 + sn],
                     dHUL[p], waits=[W(cVE, hu_ld_free[p])])

            def hladder(e, h_=hub, t=tb, sn_=sn):
                g_ = ST2[:, 48 + t:49 + t]
                e.tensor_scalar(fp(4096, sn_), h_, g_, 128.5, op0=OP.mult,
                                op1=OP.add)
                bufs = [fp(4096, sn_), fp(8192, sn_), fp(10240, sn_)]
                cur = 0
                for b_ in range(7, -1, -1):
                    s_ = bufs[cur]
                    d_ = bufs[1] if cur != 1 else bufs[2]
                    e.drain()
                    e.tensor_scalar(fp(6144, sn_), s_, float(2 ** b_),
                                    float(2 ** b_), op0=OP.is_ge, op1=OP.mult)
                    e.drain()
                    e.tensor_tensor(d_, s_, fp(6144, sn_), op=OP.subtract)
                    cur = 1 if cur != 1 else 2
                rb_ = bufs[cur]
                e.drain()
                e.tensor_scalar(fp(12288, sn_), rb_, 0.0, None, op0=OP.is_le)
                e.drain()
                e.tensor_tensor(fp(14336, sn_), fp(12288, sn_), fp(6144, sn_),
                                op=OP.mult)
                e.drain()
                e.tensor_tensor(fp(12288, sn_), rb_, fp(14336, sn_), op=OP.add)
                e.drain()
                return e.scalar_tensor_tensor(fp(6144, sn_), fp(4096, sn_),
                                              128.0, fp(12288, sn_),
                                              op0=OP.subtract, op1=OP.subtract)
            hu_ld_free[p] = E("vector", hladder, waits=[W(dHUL[p], ld)], inc=cVE)
            lq = cVE.v
            for k in range(sn // 128):
                fc = (s0 + 128 * k) // 128
                par = k % 2
                tr = E("tensor", lambda e, k_=k, par_=par: e.transpose(
                    PS[:, 512 * par_:512 * par_ + 128],
                    fp(6144 + 128 * k_, 128), identf[:]),
                    waits=[W(cVE, lq), W(cVE, pt4_free[par]), W(cVE, gu_ps_free)],
                    inc=cTE)
                pt4_free[par] = E("vector", lambda e, fc_=fc, t=tb, par_=par:
                                  e.tensor_copy(bf(1024 * fc_ + 128 * t, 128),
                                                PS[:, 512 * par_:512 * par_ + 128]),
                                  waits=[W(cTE, tr)], inc=cVE)
    hqT_done = cVE.v

    td_ld_free = [W(cVE, hqT_done), W(cVE, hqT_done)]
    sd_scrap_free = [W(cVE, hqT_done), W(cVE, hqT_done)]
    x1s_free = [hqT_done, hqT_done]
    ev_free2 = [0, 0]
    r2_tb_done = [0] * 8
    dn_ps_free = max(pt4_free)
    for ot in range(4):
        mm_prev = 0
        for fc in range(FF // 128):
            p = fc % 2
            buf = fp(512 * p, 512)
            ld = dma("sync", buf, wdT[128 * fc:128 * (fc + 1), 512 * ot:512 * (ot + 1)],
                     dTD[p], waits=[td_ld_free[p]])

            def mk_sd(e, b=buf, sid=p):
                e.activation(bf(46080 + 1024 * sid, 512), b, AF.Sign,
                             bias=SCb[:, 5:6])
                return e.activation(bf(46080 + 1024 * sid + 512, 512), b,
                                    AF.Sign, bias=SCb[:, 4:5])
            sd = E("scalar", mk_sd,
                   waits=[W(dTD[p], ld), sd_scrap_free[p]], inc=cSC)
            td_ld_free[p] = W(cSC, sd)
            tdq = E("vector", lambda e, sid=p: e.tensor_tensor(
                bf(45056 + 512 * sid, 512), bf(46080 + 1024 * sid, 512),
                bf(46080 + 1024 * sid + 512, 512), op=OP.add),
                waits=[W(cSC, sd), W(cTE, mm_prev)], inc=cVE)
            sd_scrap_free[p] = W(cVE, tdq)

            def mmd(e, fc_=fc, p_=p):
                for tb in range(8):
                    lh = bf(1024 * fc_ + 128 * tb, 128)
                    mm = e.matmul(bank(tb), lh, bf(45056 + 512 * p_, 512),
                                  start=(fc_ == 0), stop=(fc_ == FF // 128 - 1))
                return mm
            mm_prev = E("tensor", mmd,
                        waits=[W(cVE, cVE.v), W(cVE, dn_ps_free)],
                        inc=cTE)
        for tb in range(8):
            p = tb % 2
            x1s = fp(1024 + 512 * p, 512)
            ld = dma("sync", x1s, sX1[128 * tb:128 * (tb + 1), 512 * ot:512 * (ot + 1)],
                     dXSL[p], waits=[W(cVE, x1s_free[p])])
            evb = fp(2048 + 512 * p, 512)
            x1s_free[p] = E("vector", lambda e, t=tb, x=x1s, ev=evb: e.scalar_tensor_tensor(
                ev, bank(t), ST2[:, 56 + t:57 + t], x, op0=OP.mult, op1=OP.add),
                waits=[W(cTE, mm_prev), W(dXSL[p], ld), W(dR2S[p], ev_free2[p])],
                inc=cVE)
            if tb >= 6:
                dn_ps_free = cVE.v
            ev_free2[p] = dma("sync", sR2[128 * tb:128 * (tb + 1), 512 * ot:512 * (ot + 1)],
                              evb, dR2S[p], waits=[W(cVE, x1s_free[p])])
            if ot == 3:
                r2_tb_done[tb] = ev_free2[p]
    p4c_r2s_final = (dR2S[0].v, dR2S[1].v)

    rt_free2 = [0, 0]
    out_free = [0, 0]
    for tb in range(8):
        p = tb % 2
        rt = fp(3072 + 2048 * p, 2048)
        ld = dma("sync", rt, sR2[128 * tb:128 * (tb + 1), :], dR2L[p],
                 waits=[W(dR2S[tb % 2], r2_tb_done[tb]), W(cVE, rt_free2[p])])

        def ln_stats2(e, r=rt):
            e.reduce_sum(ST2[:, 28:29], r, axis=AX)
            e.drain()
            e.tensor_scalar(ST2[:, 29:30], ST2[:, 28:29], -(2.0 ** -11), None,
                            op0=OP.mult)
            return e.drain()
        lm = E("vector", ln_stats2, waits=[W(dR2L[p], ld)], inc=cVE)
        sq = E("scalar", lambda e, r=rt: e.activation(
            fp(11264, 2048), r, AF.Square, bias=ST2[:, 29:30],
            accum_out=ST2[:, 30:31]), waits=[W(cVE, lm)], inc=cSC)

        def ln_rstd2(e):
            e.tensor_scalar(ST2[:, 31:32], ST2[:, 30:31], 2.0 ** -11, 1e-5,
                            op0=OP.mult, op1=OP.add)
            return e.drain()
        v1_ = E("vector", ln_rstd2, waits=[W(cSC, sq)], inc=cVE)
        sq2 = E("scalar", lambda e: e.activation(ST2[:, 32:33], ST2[:, 31:32],
                                                 AF.Sqrt), waits=[W(cVE, v1_)],
                inc=cSC)

        def ln_fin2(e, r=rt, p_=p):
            e.reciprocal(ST2[:, 33:34], ST2[:, 32:33])
            e.drain()
            e.tensor_scalar(fp(13312, 2048), r, ST2[:, 29:30], None, op0=OP.add)
            e.drain()
            return e.tensor_scalar(fp(7168 + 2048 * p_, 2048), fp(13312, 2048),
                                   ST2[:, 33:34], None, op0=OP.mult)
        fin = E("vector", ln_fin2, waits=[W(cSC, sq2), W(dOUT[p], out_free[p])],
                inc=cVE)
        rt_free2[p] = fin
        out_free[p] = dma("sync", out[128 * tb:128 * (tb + 1), :],
                          fp(7168 + 2048 * p, 2048), dOUT[p], waits=[W(cVE, fin)])

    finals = [(c.idx, c.v) for c in all_ctrs if c.v > 0]

    nsem = Ctr._next[0]
    with ExitStack() as ctx:
        sems = [ctx.enter_context(nc.semaphore(f"sem{i}")) for i in range(nsem)]
        block = ctx.enter_context(nc.Block())

        def runner(eng, evs, tail=None, auto_drain=False):
            for waits, fn, (inci, incn) in evs:
                if auto_drain:
                    eng.drain()
                for (si, v) in waits:
                    if v > 0:
                        eng.wait_ge(sems[si], v)
                inst = fn(eng)
                if inci is not None:
                    inst.then_inc(sems[inci], incn)
            if tail:
                tail(eng)

        def sync_tail(e):
            for (si, v) in finals:
                e.wait_ge(sems[si], v)

        @block.sync
        def _(sync):
            runner(sync, events["sync"], tail=sync_tail)

        @block.tensor
        def _(tensor):
            runner(tensor, events["tensor"])

        @block.vector
        def _(vector):
            runner(vector, events["vector"], auto_drain=True)

        @block.scalar
        def _(scalar):
            runner(scalar, events["scalar"], auto_drain=True)

        @block.gpsimd
        def _(gpsimd):
            runner(gpsimd, events["gpsimd"])

    return nc


_CACHE = {}


def _host_inputs(x, w_q, w_k, w_v, w_o, w_gu, w_d, nw):
    in_maps = []
    own_all = []
    for c in range(8):
        b = c // 2
        slots = SLOTS_A if c % 2 == 0 else SLOTS_B
        own = np.concatenate([np.arange(128 * p_, 128 * (p_ + 1)) for p_ in slots])
        own_all.append((b, own))
        xTb = np.ascontiguousarray(x[b].T)
        mk = np.empty((128, 8 * 256), np.float32)
        for g, p_ in enumerate(slots):
            L = L_PROG[g]
            keys = (L - 2) * 128 + np.arange(256)
            qpos = 128 * p_ + np.arange(128)
            mk[:, 256 * g:256 * (g + 1)] = np.where(
                keys[None, :] <= qpos[:, None], np.float32(0.0), np.float32(-1e30))
        in_maps.append({
            "xT": xTb,
            "xTq": np.ascontiguousarray(xTb[:, own]),
            "xown": np.ascontiguousarray(x[b][own]),
            "wq": w_q, "wk": w_k, "wv": w_v, "wo": w_o,
            "wguT": np.ascontiguousarray(w_gu.T),
            "wdT": np.ascontiguousarray(w_d.T),
            "nw_in": np.ascontiguousarray(nw.reshape(1, D)),
            "identb_in": np.eye(128, dtype=ml_dtypes.bfloat16),
            "identf_in": np.eye(128, dtype=np.float32),
            "masks_in": mk,
        })
    return in_maps, own_all


def kernel(L0_attn_x, L0_attn_w_q, L0_attn_w_k, L0_attn_w_v, L0_attn_w_o,
           L0_ffn_w_gate_up, L0_ffn_w_down, L0_ffn_norm_weight):
    x = np.ascontiguousarray(np.asarray(L0_attn_x, dtype=np.float32))
    in_maps, own_all = _host_inputs(
        x,
        np.ascontiguousarray(np.asarray(L0_attn_w_q, np.float32)),
        np.ascontiguousarray(np.asarray(L0_attn_w_k, np.float32)),
        np.ascontiguousarray(np.asarray(L0_attn_w_v, np.float32)),
        np.ascontiguousarray(np.asarray(L0_attn_w_o, np.float32)),
        np.ascontiguousarray(np.asarray(L0_ffn_w_gate_up, np.float32)),
        np.ascontiguousarray(np.asarray(L0_ffn_w_down, np.float32)),
        np.ascontiguousarray(np.asarray(L0_ffn_norm_weight, np.float32)),
    )
    if "nc" not in _CACHE:
        _CACHE["nc"] = build_program()
    res = run_bass_kernel_spmd(_CACHE["nc"], in_maps, core_ids=list(range(8)))
    y = np.empty((B, S, D), dtype=np.asarray(res.results[0]["out"]).dtype)
    for c in range(8):
        b, own = own_all[c]
        y[b, own, :] = res.results[c]["out"]
    return y


# version 19
# speedup vs baseline: 1.0127x; 1.0127x over previous
import numpy as np
import ml_dtypes
from contextlib import ExitStack

import concourse.bass as bass
import concourse.mybir as mybir
from concourse.bass_utils import run_bass_kernel_spmd

F32 = mybir.dt.float32
BF16 = mybir.dt.bfloat16
OP = mybir.AluOpType
AF = mybir.ActivationFunctionType
AX = mybir.AxisListType.X

B, S, D = 4, 2048, 2048
NH, DH = 16, 128
FF = 5632
TOK = 1024
NCH = D // 128
L_PROG = [16, 14, 12, 10, 8, 6, 4, 2]
SLOTS_A = [15, 13, 11, 9, 7, 5, 3, 1]
SLOTS_B = [14, 12, 10, 8, 6, 4, 2, 0]

SC_A = float(np.float32(1.0) / np.float32(np.sqrt(np.float64(128.0))))
N_GU = 2048 * 11264
N_D = 5632 * 2048

ENGS = ("sync", "tensor", "vector", "scalar", "gpsimd")
DEBUG = False


class Ctr:
    _next = [0]

    def __init__(self, name):
        self.idx = Ctr._next[0]
        Ctr._next[0] += 1
        self.name = name
        self.v = 0


def build_program():
    Ctr._next[0] = 0
    nc = bass.Bass("TRN2", target_bir_lowering=False)

    din = lambda n, sh, dt=F32: nc.dram_tensor(n, sh, dt, kind="ExternalInput").ap()
    dsc = lambda n, sh, dt=F32: nc.dram_tensor(
        n, sh, dt, kind="ExternalOutput" if DEBUG else "Internal").ap()
    xT = din("xT", [D, S])
    xTq = din("xTq", [D, TOK])
    xown = din("xown", [TOK, D])
    wq = din("wq", [D, D]); wk = din("wk", [D, D])
    wv = din("wv", [D, D]); wo = din("wo", [D, D])
    wguT = din("wguT", [D, 2 * FF])
    wdT = din("wdT", [FF, D])
    nw_in = din("nw_in", [1, D])
    identb_in = din("identb_in", [128, 128], BF16)
    identf_in = din("identf_in", [128, 128], F32)
    masks_in = din("masks_in", [128, 8 * 256], F32)
    out = nc.dram_tensor("out", [TOK, D], F32, kind="ExternalOutput").ap()

    sKTh = dsc("sKTh", [D, S], BF16); sKTl = dsc("sKTl", [D, S], BF16)
    sVh = dsc("sVh", [S, D], BF16); sVl = dsc("sVl", [S, D], BF16)
    sQTh = dsc("sQTh", [D, TOK], BF16); sQTl = dsc("sQTl", [D, TOK], BF16)
    sX1 = dsc("sX1", [TOK, D]); sR1 = dsc("sR1", [TOK, D])
    sHU = dsc("sHU", [TOK, FF]); sR2 = dsc("sR2", [TOK, D])
    sSC = dsc("sSC", [1, 8])

    sb = lambda n, sh, dt=F32: nc.alloc_sbuf_tensor(n, sh, dt).ap()
    identb = sb("identb", [128, 128], BF16)
    identf = sb("identf", [128, 128], F32)
    masks = sb("masks", [128, 8 * 256], F32)
    nw = sb("nw", [128, D], F32)
    ones1 = sb("ones1", [128, 1], F32)
    ST = sb("ST", [128, 160], F32)
    ST2 = sb("ST2", [128, 64], F32)
    SCb = sb("SCb", [128, 8], F32)
    ABF = sb("ABF", [128, 57344], BF16)
    AFP = sb("AFP", [128, 16384], F32)
    PS = nc.alloc_psum_tensor("PS", [128, 4096], F32).ap()

    bf = lambda o, n: ABF[:, o:o + n]
    fp = lambda o, n: AFP[:, o:o + n]
    bank = lambda i: PS[:, 512 * i:512 * (i + 1)]

    events = {e: [] for e in ENGS}
    all_ctrs = []

    def C(name):
        c = Ctr(name)
        all_ctrs.append(c)
        return c

    cTE, cVE, cSC = C("TE"), C("VE"), C("SC")
    dCONST = C("dCONST")
    dMEAN = [C("dMEAN0"), C("dMEAN1"), C("dMEAN2")]
    dXQ = [C("dXQ0"), C("dXQ1")]
    dWQ = [C("dWQ0"), C("dWQ1")]
    dQS = [C("dQS0"), C("dQS1")]
    dKS = [C("dKS0"), C("dKS1")]
    dSC1, dSC2 = C("dSC1"), C("dSC2")
    dHD = [C("dHD0"), C("dHD1")]
    dWO = [C("dWO0"), C("dWO1")]
    dXO = [C("dXO0"), C("dXO1")]
    dRS = [C("dRS0"), C("dRS1")]
    dR1L = [C("dR1L0"), C("dR1L1")]
    dX1S = [C("dX1S0"), C("dX1S1")]
    dX1L = [C("dX1L0"), C("dX1L1")]
    dGU = [C("dGU0"), C("dGU1"), C("dGU2"), C("dGU3")]
    dHUS = [C("dHUS0"), C("dHUS1")]
    dHUL = [C("dHUL0"), C("dHUL1")]
    dTD = [C("dTD0"), C("dTD1")]
    dXSL = [C("dXSL0"), C("dXSL1")]
    dR2S = [C("dR2S0"), C("dR2S1")]
    dR2L = [C("dR2L0"), C("dR2L1")]
    dOUT = [C("dOUT0"), C("dOUT1")]

    def E(eng, fn, waits=(), inc=None, n=1):
        events[eng].append((tuple(waits), fn, (inc.idx if inc else None, n)))
        if inc is not None:
            inc.v += n
        return inc.v if inc is not None else None

    def dma(eng_name, dst, src, c, waits=()):
        return E(eng_name, lambda e, d=dst, s=src: e.dma_start(out=d, in_=s),
                 waits=waits, inc=c, n=16)

    W = lambda c, v=None: (c.idx, c.v if v is None else v)

    dma("sync", identb[:], identb_in, dCONST)
    dma("sync", identf[:], identf_in, dCONST)
    dma("sync", masks[:], masks_in, dCONST)
    dma("sync", nw[:], nw_in.broadcast_to([128, D]), dCONST)
    const_done = dCONST.v
    E("vector", lambda e: e.memset(ones1[:], 1.0), inc=cVE)
    E("vector", lambda e: e.memset(ST[:], 0.0), inc=cVE)
    st_zero = cVE.v

    mean_tiles = []
    col = 0
    for i in range(NCH):
        for j0 in range(0, 2 * FF, 2048):
            ncol = min(2048, 2 * FF - j0)
            mean_tiles.append((wguT[128 * i:128 * (i + 1), j0:j0 + ncol], ncol, col))
            col += 1
    n_gu_tiles = col
    for i in range(FF // 128):
        mean_tiles.append((wdT[128 * i:128 * (i + 1), :], 2048, col))
        col += 1
    n_mean_tiles = col

    mean_sc_free = [0, 0, 0]
    mean_idx = [0]

    def emit_mean(n):
        for _ in range(n):
            k = mean_idx[0]
            if k >= len(mean_tiles):
                return
            mean_idx[0] = k + 1
            msrc, ncol, c_ = mean_tiles[k]
            buf = fp(4096 + 2048 * (k % 3), ncol)
            scrap = fp(12288, ncol)
            ld = dma("gpsimd", buf, msrc, dMEAN[k % 3],
                     waits=[W(cSC, mean_sc_free[k % 3]), (cVE.idx, st_zero)])
            mean_sc_free[k % 3] = E(
                "scalar",
                lambda e, b=buf, cc=c_, s_=scrap: e.activation(
                    s_, b, AF.Abs, accum_out=ST[:, cc:cc + 1]),
                waits=[W(dMEAN[k % 3], ld)], inc=cSC)

    ld_free = [0, 0]
    for i in range(NCH):
        p = i % 2
        buf = fp(1024 + 1024 * p, 1024)
        ld = dma("sync", buf, xTq[128 * i:128 * (i + 1), :], dXQ[p],
                 waits=[W(cVE, ld_free[p])])
        hc = E("scalar", lambda e, b=buf, i_=i: e.activation(
            bf(1024 * i_, 1024), b, AF.Copy), waits=[W(dXQ[p], ld)], inc=cSC)
        ld_free[p] = E("vector",
                       lambda e, b=buf, i_=i: e.tensor_tensor(
                           bf(16384 + 1024 * i_, 1024), b, bf(1024 * i_, 1024),
                           op=OP.subtract), waits=[W(cSC, hc)], inc=cVE)
    xq_ready = cVE.v
    xq_ready_sc = cSC.v

    qev_free = [0, 0]
    ps_evac_done = 0
    wq_ld_free = [0, 0]
    prev_qmm = 0
    for g in range(4):
        w_ready = []
        for i in range(NCH):
            p = i % 2
            buf = fp(512 * p, 512)
            ld = dma("sync", buf, wq[128 * i:128 * (i + 1), 512 * g:512 * (g + 1)],
                     dWQ[p], waits=[W(cVE, wq_ld_free[p])])
            hc = E("scalar", lambda e, b=buf, i_=i: e.activation(
                bf(32768 + 512 * i_, 512), b, AF.Copy),
              waits=[W(dWQ[p], ld), W(cTE, prev_qmm)], inc=cSC)
            wq_ld_free[p] = E("vector",
                              lambda e, b=buf, i_=i: e.tensor_tensor(
                                  bf(40960 + 512 * i_, 512), b,
                                  bf(32768 + 512 * i_, 512), op=OP.subtract),
                              waits=[W(cSC, hc)], inc=cVE)
            w_ready.append(cVE.v)

        def mk_mmq(i):
            def mmq(e, i_=i):
                wh = bf(32768 + 512 * i_, 512)
                wl = bf(40960 + 512 * i_, 512)
                for jb in range(4):
                    lh = wh[:, 128 * jb:128 * (jb + 1)]
                    ll = wl[:, 128 * jb:128 * (jb + 1)]
                    for t2 in range(2):
                        xh = bf(1024 * i_ + 512 * t2, 512)
                        xl = bf(16384 + 1024 * i_ + 512 * t2, 512)
                        bk = PS[:, 1024 * jb + 512 * t2: 1024 * jb + 512 * t2 + 512]
                        e.matmul(bk, lh, xh, start=(i_ == 0), stop=False)
                        e.matmul(bk, lh, xl, start=False, stop=False)
                        mm = e.matmul(bk, ll, xh, start=False, stop=(i_ == NCH - 1))
                return mm
            return mmq
        for i in range(NCH):
            wts = [W(cVE, w_ready[i])]
            if i == 0:
                wts += [W(cVE, xq_ready), W(cVE, ps_evac_done)]
            q_mm = E("tensor", mk_mmq(i), waits=wts, inc=cTE)
        prev_qmm = q_mm
        for jb in range(4):
            p = jb % 2
            pcol = PS[:, 1024 * jb:1024 * (jb + 1)]
            t2 = fp(11264, 1024)
            qh = bf(49152 + 2048 * p, 1024)
            ql = bf(49152 + 2048 * p + 1024, 1024)
            E("vector", lambda e, pc=pcol, tb=t2: e.tensor_scalar(
                tb, pc, SC_A, None, op0=OP.mult),
              waits=[W(cTE, q_mm), W(dQS[p], qev_free[p])], inc=cVE)
            E("vector", lambda e, tb=t2, q=qh: e.tensor_copy(q, tb), inc=cVE)
            ev = E("vector", lambda e, tb=t2, q=qh, l=ql: e.tensor_tensor(
                l, tb, q, op=OP.subtract), inc=cVE)
            if jb >= 2:
                ps_evac_done = ev
            r0 = 512 * g + 128 * jb
            dma("sync", sQTh[r0:r0 + 128, :], qh, dQS[p], waits=[W(cVE, ev)])
            qev_free[p] = dma("sync", sQTl[r0:r0 + 128, :], ql, dQS[p])
        emit_mean(7)
    p1a_qs_final = (dQS[0].v, dQS[1].v)

    ev_first = [True, True]
    ev_free = [0, 0]
    w_ld_free = [max(wq_ld_free), max(wq_ld_free)]
    kv_ps_free = ps_evac_done
    prev_kvmm = prev_qmm
    for th in range(2):
        for i in range(NCH):
            p = i % 2
            buf = fp(1024 + 1024 * p, 1024)
            ld = dma("sync", buf, xT[128 * i:128 * (i + 1), 1024 * th:1024 * (th + 1)],
                     dXQ[p], waits=[W(cVE, ld_free[p])])
            hc = E("scalar", lambda e, b=buf, i_=i: e.activation(
                bf(1024 * i_, 1024), b, AF.Copy), waits=[W(dXQ[p], ld)], inc=cSC)
            ld_free[p] = E("vector",
                           lambda e, b=buf, i_=i: e.tensor_tensor(
                               bf(16384 + 1024 * i_, 1024), b, bf(1024 * i_, 1024),
                               op=OP.subtract), waits=[W(cSC, hc)], inc=cVE)
        x_ready = cVE.v

        for g in range(4):
            w_ready = []
            for i in range(NCH):
                p = i % 2
                buf = fp(512 * p, 512)
                ld = dma("sync", buf, wk[128 * i:128 * (i + 1), 512 * g:512 * (g + 1)],
                         dWQ[p], waits=[W(cVE, w_ld_free[p])])
                hc = E("scalar", lambda e, b=buf, i_=i: e.activation(
                    bf(32768 + 512 * i_, 512), b, AF.Copy),
                  waits=[W(dWQ[p], ld), W(cTE, prev_kvmm)], inc=cSC)
                w_ld_free[p] = E("vector",
                                 lambda e, b=buf, i_=i: e.tensor_tensor(
                                     bf(40960 + 512 * i_, 512), b,
                                     bf(32768 + 512 * i_, 512), op=OP.subtract),
                                 waits=[W(cSC, hc)], inc=cVE)
                w_ready.append(cVE.v)

            def mk_mmk(i):
                def mmk(e, i_=i):
                    for jb in range(4):
                        lh = bf(32768 + 512 * i_, 512)[:, 128 * jb:128 * (jb + 1)]
                        ll = bf(40960 + 512 * i_, 512)[:, 128 * jb:128 * (jb + 1)]
                        for t2 in range(2):
                            xh = bf(1024 * i_ + 512 * t2, 512)
                            xl = bf(16384 + 1024 * i_ + 512 * t2, 512)
                            bk = PS[:, 1024 * jb + 512 * t2:1024 * jb + 512 * t2 + 512]
                            e.matmul(bk, lh, xh, start=(i_ == 0), stop=False)
                            e.matmul(bk, lh, xl, start=False, stop=False)
                            mm = e.matmul(bk, ll, xh, start=False, stop=(i_ == NCH - 1))
                    return mm
                return mmk
            for i in range(NCH):
                wts = [W(cVE, w_ready[i])]
                if i == 0:
                    wts += [W(cVE, x_ready), W(cVE, kv_ps_free)]
                k_mm = E("tensor", mk_mmk(i), waits=wts, inc=cTE)
            prev_kvmm = k_mm
            for jb in range(4):
                p = jb % 2
                pcol = PS[:, 1024 * jb:1024 * (jb + 1)]
                kh = bf(49152 + 2048 * p, 1024)
                kl = bf(49152 + 2048 * p + 1024, 1024)
                evw = [W(cTE, k_mm)]
                if ev_first[p]:
                    evw.append(W(dQS[p], p1a_qs_final[p]))
                    ev_first[p] = False
                else:
                    evw.append(W(dKS[p], ev_free[p]))
                hc = E("scalar", lambda e, pc=pcol, q=kh: e.activation(q, pc, AF.Copy),
                  waits=evw, inc=cSC)
                ev = E("vector", lambda e, pc=pcol, q=kh, l=kl: e.tensor_tensor(
                    l, pc, q, op=OP.subtract), waits=[W(cSC, hc)], inc=cVE)
                if jb >= 2:
                    kv_ps_free = ev
                r0 = 512 * g + 128 * jb
                dma("sync", sKTh[r0:r0 + 128, 1024 * th:1024 * (th + 1)], kh,
                    dKS[p], waits=[W(cVE, ev)])
                ev_free[p] = dma("sync", sKTl[r0:r0 + 128, 1024 * th:1024 * (th + 1)],
                                 kl, dKS[p])
            emit_mean(7)

        for jq in range(4):
            w_ready = []
            for i in range(NCH):
                p = i % 2
                buf = fp(512 * p, 512)
                ld = dma("sync", buf, wv[128 * i:128 * (i + 1), 512 * jq:512 * (jq + 1)],
                         dWQ[p], waits=[W(cVE, w_ld_free[p])])
                hc = E("scalar", lambda e, b=buf, i_=i: e.activation(
                    bf(32768 + 512 * i_, 512), b, AF.Copy),
                  waits=[W(dWQ[p], ld), W(cTE, prev_kvmm)], inc=cSC)
                w_ld_free[p] = E("vector",
                                 lambda e, b=buf, i_=i: e.tensor_tensor(
                                     bf(40960 + 512 * i_, 512), b,
                                     bf(32768 + 512 * i_, 512), op=OP.subtract),
                                 waits=[W(cSC, hc)], inc=cVE)
                w_ready.append(cVE.v)

            def mk_mmv(i):
                def mmv(e, i_=i):
                    for tb in range(8):
                        lh = bf(1024 * i_, 1024)[:, 128 * tb:128 * (tb + 1)]
                        ll = bf(16384 + 1024 * i_, 1024)[:, 128 * tb:128 * (tb + 1)]
                        wh = bf(32768 + 512 * i_, 512)
                        wl = bf(40960 + 512 * i_, 512)
                        bk = bank(tb)
                        e.matmul(bk, lh, wh, start=(i_ == 0), stop=False)
                        e.matmul(bk, lh, wl, start=False, stop=False)
                        mm = e.matmul(bk, ll, wh, start=False, stop=(i_ == NCH - 1))
                    return mm
                return mmv
            for i in range(NCH):
                wts = [W(cVE, w_ready[i])]
                if i == 0:
                    wts += [W(cVE, x_ready), W(cVE, kv_ps_free)]
                v_mm = E("tensor", mk_mmv(i), waits=wts, inc=cTE)
            prev_kvmm = v_mm
            for tb in range(8):
                p = tb % 2
                vh = bf(49152 + 2048 * p, 512)
                vl = bf(49152 + 2048 * p + 512, 512)
                hc = E("scalar", lambda e, t=tb, q=vh: e.activation(q, bank(t), AF.Copy),
                  waits=[W(cTE, v_mm), W(dKS[p], ev_free[p])], inc=cSC)
                ev = E("vector", lambda e, t=tb, q=vh, l=vl: e.tensor_tensor(
                    l, bank(t), q, op=OP.subtract), waits=[W(cSC, hc)], inc=cVE)
                if tb >= 6:
                    kv_ps_free = ev
                r0 = 1024 * th + 128 * tb
                dma("sync", sVh[r0:r0 + 128, 512 * jq:512 * (jq + 1)], vh,
                    dKS[p], waits=[W(cVE, ev)])
                ev_free[p] = dma("sync", sVl[r0:r0 + 128, 512 * jq:512 * (jq + 1)],
                                 vl, dKS[p])
            emit_mean(7)
    p1_ks_final = (dKS[0].v, dKS[1].v)
    emit_mean(len(mean_tiles))
    assert mean_idx[0] == len(mean_tiles)
    mean_acts_done = cSC.v

    def mean_stats(e):
        e.reduce_sum(ST2[:, 0:1], ST[:, 0:n_gu_tiles], axis=AX)
        e.reduce_sum(ST2[:, 1:2], ST[:, n_gu_tiles:n_mean_tiles], axis=AX)
        return e.drain()
    ms1 = E("vector", mean_stats, waits=[W(cSC, mean_acts_done)], inc=cVE)
    msum = E("tensor", lambda e: e.matmul(PS[0:1, 0:2], ones1[:], ST2[:, 0:2],
                                          start=True, stop=True),
             waits=[W(cVE, ms1), W(cVE, kv_ps_free)], inc=cTE)

    def mean_math(e):
        r = ST2[0:1, :]
        e.tensor_copy(r[:, 4:6], PS[0:1, 0:2])
        e.drain()
        e.memset(r[:, 6:7], float(N_GU))
        e.memset(r[:, 7:8], float(N_D))
        e.drain()
        e.reciprocal(r[:, 8:9], r[:, 6:7])
        e.reciprocal(r[:, 9:10], r[:, 7:8])
        e.drain()
        e.tensor_tensor(r[:, 10:11], r[:, 4:5], r[:, 8:9], op=OP.mult)
        e.tensor_tensor(r[:, 11:12], r[:, 5:6], r[:, 9:10], op=OP.mult)
        e.drain()
        e.tensor_scalar(r[:, 12:13], r[:, 10:11], 1e-5, None, op0=OP.max)
        e.tensor_scalar(r[:, 13:14], r[:, 11:12], 1e-5, None, op0=OP.max)
        e.drain()
        e.tensor_scalar(r[:, 16:17], r[:, 12:13], 0.5, None, op0=OP.mult)
        e.tensor_scalar(r[:, 20:21], r[:, 13:14], 0.5, None, op0=OP.mult)
        e.reciprocal(r[:, 15:16], r[:, 12:13])
        e.tensor_scalar(r[:, 19:20], r[:, 13:14], 0.5, None, op0=OP.mult)
        e.drain()
        e.tensor_scalar(r[:, 18:19], r[:, 15:16], 2.0, None, op0=OP.mult)
        e.tensor_scalar(r[:, 17:18], r[:, 16:17], -1.0, None, op0=OP.mult)
        e.tensor_scalar(r[:, 21:22], r[:, 20:21], -1.0, None, op0=OP.mult)
        return e.drain()
    ms2 = E("vector", mean_math, waits=[W(cTE, msum)], inc=cVE)
    dma("sync", sSC[0:1, 0:6], ST2[0:1, 16:22], dSC1, waits=[W(cVE, ms2)])
    dma("sync", SCb[:, 0:6], sSC[0:1, 0:6].broadcast_to([128, 6]), dSC2,
        waits=[W(dSC1, 16)])
    scb_done = dSC2.v

    hb_free = [0, 0]
    s_banks_free = 0
    m_bank_free = 0
    pt_free = [0, 0]
    av_free2 = [0]
    p_free = ms2
    first_smm = [True]
    first_odd_ps = [True]

    for h in range(NH):
        p = h % 2
        base = 32768 + 10240 * p
        kth = bf(base, 2048); ktl = bf(base + 2048, 2048)
        vh = bf(base + 4096, 2048); vl = bf(base + 6144, 2048)
        qth = bf(base + 8192, 1024); qtl = bf(base + 9216, 1024)
        hw_waits = [W(cTE, hb_free[p]),
                    W(dQS[0], p1a_qs_final[0]), W(dQS[1], p1a_qs_final[1]),
                    W(dKS[0], p1_ks_final[0]), W(dKS[1], p1_ks_final[1])]
        dma("sync", kth, sKTh[128 * h:128 * (h + 1), :], dHD[p], waits=hw_waits)
        dma("sync", ktl, sKTl[128 * h:128 * (h + 1), :], dHD[p])
        vsrc_h = sVh.rearrange("(tb p) j -> p tb j", p=128)[:, :, 128 * h:128 * (h + 1)]
        vsrc_l = sVl.rearrange("(tb p) j -> p tb j", p=128)[:, :, 128 * h:128 * (h + 1)]
        dma("sync", vh.rearrange("p (tb d) -> p tb d", d=128), vsrc_h, dHD[p])
        dma("sync", vl.rearrange("p (tb d) -> p tb d", d=128), vsrc_l, dHD[p])
        dma("sync", qth, sQTh[128 * h:128 * (h + 1), :], dHD[p])
        hload = dma("sync", qtl, sQTl[128 * h:128 * (h + 1), :], dHD[p])

        pending = None

        def back(pend):
            g_, L_, ps_v, vh_, vl_, hb_slot, pb_ = pend
            for kj in range(L_):
                par = kj % 2
                tr = E("tensor", lambda e, k=kj, par_=par, pb=pb_: e.transpose(
                    PS[:, 2560 + 512 * par_:2560 + 512 * par_ + 128],
                    fp(pb + 128 * k, 128), identf[:]),
                    waits=[W(cVE, ps_v), W(cVE, pt_free[par])], inc=cTE)
                hc = E("scalar", lambda e, k=kj, par_=par: e.activation(
                    bf(53248 + 128 * k, 128),
                    PS[:, 2560 + 512 * par_:2560 + 512 * par_ + 128], AF.Copy),
                  waits=[W(cTE, tr)], inc=cSC)
                pt_free[par] = E("vector", lambda e, k=kj, par_=par: e.tensor_tensor(
                    bf(55296 + 128 * k, 128),
                    PS[:, 2560 + 512 * par_:2560 + 512 * par_ + 128],
                    bf(53248 + 128 * k, 128), op=OP.subtract),
                    waits=[W(cSC, hc)], inc=cVE)
            pt_done = cVE.v

            def mpv(e, L__=L_, vh__=vh_, vl__=vl_):
                for kj in range(L__):
                    lv_h = vh__[:, 128 * kj:128 * (kj + 1)]
                    lv_l = vl__[:, 128 * kj:128 * (kj + 1)]
                    rh = bf(53248 + 128 * kj, 128)
                    rl = bf(55296 + 128 * kj, 128)
                    bk = PS[:, 3584:3712]
                    e.matmul(bk, lv_h, rh, start=(kj == 0), stop=False)
                    e.matmul(bk, lv_h, rl, start=False, stop=False)
                    mm = e.matmul(bk, lv_l, rh, start=False, stop=(kj == L__ - 1))
                return mm
            pv_mm = E("tensor", mpv, waits=[W(cVE, pt_done), W(cVE, av_free2[0])],
                      inc=cTE)
            if hb_slot is not None:
                hb_free[hb_slot] = pv_mm
            hc = E("scalar", lambda e, h_=h, g__=g_: e.activation(
                bf(1024 * h_ + 128 * g__, 128), PS[:, 3584:3712], AF.Copy),
              waits=[W(cTE, pv_mm)], inc=cSC)
            av_free2[0] = E("vector", lambda e, h_=h, g__=g_: e.tensor_tensor(
                bf(16384 + 1024 * h_ + 128 * g__, 128), PS[:, 3584:3712],
                bf(1024 * h_ + 128 * g__, 128), op=OP.subtract),
                waits=[W(cSC, hc)], inc=cVE)

        for g in range(8):
            L = L_PROG[g]
            W_ = L * 128
            qh_ = qth[:, 128 * g:128 * (g + 1)]
            ql_ = qtl[:, 128 * g:128 * (g + 1)]

            def mms(e, L_=L, qh=qh_, ql=ql_, kth_=kth, ktl_=ktl):
                n_left = L_ * 128
                o = 0
                while n_left > 0:
                    n = min(512, n_left)
                    bk = PS[:, o:o + n]
                    e.matmul(bk, qh, kth_[:, o:o + n], start=True, stop=False)
                    e.matmul(bk, qh, ktl_[:, o:o + n], start=False, stop=False)
                    mm = e.matmul(bk, ql, kth_[:, o:o + n], start=False, stop=True)
                    o += n; n_left -= n
                return mm
            smw = [W(dHD[p], hload), W(cSC, s_banks_free)]
            if first_smm[0]:
                smw.append(W(cVE, p_free))
                first_smm[0] = False
            s_mm = E("tensor", mms, waits=smw, inc=cTE)
            mk = E("vector", lambda e, L_=L, g_=g: e.tensor_tensor(
                PS[:, 2048:2304], PS[:, (L_ - 2) * 128:L_ * 128],
                masks[:, 256 * g_:256 * (g_ + 1)], op=OP.add),
                waits=[W(cTE, s_mm), W(cSC, m_bank_free), W(dCONST, const_done)],
                inc=cVE)
            c0, c1 = 24, 25
            if L > 2:
                E("scalar", lambda e, L_=L: e.activation(
                    fp(0, (L_ - 2) * 128), PS[:, 0:(L_ - 2) * 128], AF.Exp,
                    accum_out=ST2[:, c0:c0 + 1]),
                  waits=[W(cVE, mk), W(cVE, p_free)], inc=cSC)
                s_banks_free = cSC.v
            m_bank_free = E("scalar", lambda e, L_=L: e.activation(
                fp((L_ - 2) * 128, 256), PS[:, 2048:2304], AF.Exp,
                accum_out=ST2[:, c1:c1 + 1]),
                waits=[W(cVE, mk)] + ([] if L > 2 else [W(cVE, p_free)]), inc=cSC)
            if L == 2:
                s_banks_free = m_bank_free
            exp_done = cSC.v

            def den_rcp(e, L_=L):
                if L_ > 2:
                    e.tensor_tensor(ST2[:, 26:27], ST2[:, c0:c0 + 1],
                                    ST2[:, c1:c1 + 1], op=OP.add)
                else:
                    e.tensor_copy(ST2[:, 26:27], ST2[:, c1:c1 + 1])
                e.drain()
                e.reciprocal(ST2[:, 27:28], ST2[:, 26:27])
                return e.drain()
            E("vector", den_rcp, waits=[W(cSC, exp_done)], inc=cVE)
            psbase = 2048 + 2048 * (g % 2)
            psw = [W(cSC, mean_acts_done)] if first_odd_ps[0] and g % 2 == 1 else []
            if g % 2 == 1:
                first_odd_ps[0] = False
            ps_ = E("vector", lambda e, W__=W_, pb=psbase: e.tensor_scalar(
                fp(pb, W__), fp(0, W__), ST2[:, 27:28], None, op0=OP.mult),
                waits=psw, inc=cVE)
            p_free = ps_

            if pending is not None:
                back(pending)
            pending = (g, L, ps_, vh, vl, p if g == 7 else None, psbase)
        back(pending)
    attnT_done = cVE.v
    attnT_done = cVE.v

    wo_free = [attnT_done, attnT_done]
    xo_free = [attnT_done, attnT_done]
    rs_free = [0, 0]
    o_ps_free = av_free2[0]
    for jt in range(4):
        mm_prev = 0
        for i in range(NCH):
            p = i % 2
            buf = fp(512 * p, 512)
            ld = dma("sync", buf, wo[128 * i:128 * (i + 1), 512 * jt:512 * (jt + 1)],
                     dWO[p], waits=[W(cVE, wo_free[p])])
            hc = E("scalar", lambda e, b=buf, p_=p: e.activation(
                bf(32768 + 512 * p_, 512), b, AF.Copy),
              waits=[W(dWO[p], ld), W(cTE, mm_prev)], inc=cSC)
            wo_free[p] = E("vector", lambda e, b=buf, p_=p: e.tensor_tensor(
                bf(33792 + 512 * p_, 512), b, bf(32768 + 512 * p_, 512),
                op=OP.subtract), waits=[W(cSC, hc)], inc=cVE)

            def mmo(e, i_=i, p_=p):
                wh = bf(32768 + 512 * p_, 512)
                wl = bf(33792 + 512 * p_, 512)
                for tb in range(8):
                    ah = bf(1024 * i_ + 128 * tb, 128)
                    al = bf(16384 + 1024 * i_ + 128 * tb, 128)
                    bk = bank(tb)
                    e.matmul(bk, ah, wh, start=(i_ == 0), stop=False)
                    e.matmul(bk, ah, wl, start=False, stop=False)
                    mm = e.matmul(bk, al, wh, start=False, stop=(i_ == NCH - 1))
                return mm
            mm_prev = E("tensor", mmo,
                        waits=[W(cVE, wo_free[p]), W(cVE, attnT_done),
                               W(cVE, o_ps_free)], inc=cTE)
        for tb in range(8):
            p = tb % 2
            xo = fp(1024 + 512 * p, 512)
            ld = dma("sync", xo, xown[128 * tb:128 * (tb + 1), 512 * jt:512 * (jt + 1)],
                     dXO[p], waits=[W(cVE, xo_free[p])])
            rt = fp(5120 + 512 * p, 512)
            xo_free[p] = E("vector", lambda e, t=tb, x=xo, r=rt: e.tensor_tensor(
                r, bank(t), x, op=OP.add),
                waits=[W(cTE, mm_prev), W(dXO[p], ld), W(dRS[p], rs_free[p])],
                inc=cVE)
            if tb >= 6:
                o_ps_free = cVE.v
            rs_free[p] = dma("sync", sR1[128 * tb:128 * (tb + 1), 512 * jt:512 * (jt + 1)],
                             rt, dRS[p], waits=[W(cVE, xo_free[p])])
    p3_rs_final = (dRS[0].v, dRS[1].v)
    p3_last_mm = mm_prev

    rt_free = [0, 0]
    x1_free = [0, 0]
    for tb in range(8):
        p = tb % 2
        rt = fp(6144 + 2048 * p, 2048)
        ld = dma("sync", rt, sR1[128 * tb:128 * (tb + 1), :], dR1L[p],
                 waits=[W(dRS[0], p3_rs_final[0]), W(dRS[1], p3_rs_final[1]),
                        W(cVE, rt_free[p])])

        def ln_stats(e, r=rt):
            e.reduce_sum(ST2[:, 28:29], r, axis=AX)
            e.drain()
            e.tensor_scalar(ST2[:, 29:30], ST2[:, 28:29], -(2.0 ** -11), None,
                            op0=OP.mult)
            return e.drain()
        lm = E("vector", ln_stats, waits=[W(dR1L[p], ld)], inc=cVE)
        sq = E("scalar", lambda e, r=rt: e.activation(
            fp(2048, 2048), r, AF.Square, bias=ST2[:, 29:30],
            accum_out=ST2[:, 30:31]), waits=[W(cVE, lm)], inc=cSC)

        def ln_rstd1(e):
            e.tensor_scalar(ST2[:, 31:32], ST2[:, 30:31], 2.0 ** -11, 1e-5,
                            op0=OP.mult, op1=OP.add)
            return e.drain()
        v1_ = E("vector", ln_rstd1, waits=[W(cSC, sq)], inc=cVE)
        sq2 = E("scalar", lambda e: e.activation(ST2[:, 32:33], ST2[:, 31:32],
                                                 AF.Sqrt), waits=[W(cVE, v1_)],
                inc=cSC)

        def ln_fin(e, r=rt, p_=p):
            e.reciprocal(ST2[:, 33:34], ST2[:, 32:33])
            e.drain()
            x1t = fp(10240 + 2048 * p_, 2048)
            e.tensor_scalar(fp(14336, 2048), r, ST2[:, 29:30], None, op0=OP.add)
            e.drain()
            return e.tensor_scalar(x1t, fp(14336, 2048), ST2[:, 33:34], None,
                                   op0=OP.mult)
        fin = E("vector", ln_fin, waits=[W(cSC, sq2), W(dX1S[p], x1_free[p])],
                inc=cVE)
        rt_free[p] = fin
        x1_free[p] = dma("sync", sX1[128 * tb:128 * (tb + 1), :],
                         fp(10240 + 2048 * p, 2048), dX1S[p], waits=[W(cVE, fin)])
    p3_x1s_final = (dX1S[0].v, dX1S[1].v)
    ln1_last_fin = rt_free[1]

    x1t_free = [0, 0]
    pt4_free = [0, 0]
    prev_ladder = 0
    gamcol = 8
    for tb in range(8):
        p = tb % 2
        x1t = fp(2048 * p, 2048)
        ld = dma("sync", x1t, sX1[128 * tb:128 * (tb + 1), :], dX1L[p],
                 waits=[W(dX1S[0], p3_x1s_final[0]), W(dX1S[1], p3_x1s_final[1]),
                        W(cVE, x1t_free[p])])
        sq = E("scalar", lambda e, x=x1t: e.activation(
            fp(12288, 2048), x, AF.Square, accum_out=ST2[:, 34:35]),
            waits=[W(dX1L[p], ld), W(cVE, prev_ladder)], inc=cSC)

        def rms1(e):
            e.tensor_scalar(ST2[:, 35:36], ST2[:, 34:35], 2.0 ** -11, 1e-5,
                            op0=OP.mult, op1=OP.add)
            return e.drain()
        v1_ = E("vector", rms1, waits=[W(cSC, sq)], inc=cVE)
        s2 = E("scalar", lambda e: e.activation(ST2[:, 36:37], ST2[:, 35:36],
                                                AF.Sqrt), waits=[W(cVE, v1_)],
               inc=cSC)

        def rms2(e, x=x1t, t=tb):
            e.reciprocal(ST2[:, 37:38], ST2[:, 36:37])
            e.drain()
            e.tensor_scalar(fp(6144, 2048), x, ST2[:, 37:38], None, op0=OP.mult)
            e.drain()
            e.tensor_tensor(fp(4096, 2048), fp(6144, 2048), nw[:], op=OP.mult)
            e.drain()
            e.reduce_max(ST2[:, 38:39], fp(4096, 2048), axis=AX,
                         apply_absolute_value=True)
            e.drain()
            e.tensor_scalar(ST2[:, 39:40], ST2[:, 38:39], 1e-5, None, op0=OP.max)
            e.drain()
            e.reciprocal(ST2[:, 40:41], ST2[:, 39:40])
            e.drain()
            e.tensor_scalar(ST2[:, gamcol + t:gamcol + t + 1], ST2[:, 40:41],
                            127.0, None, op0=OP.mult)
            return e.drain()
        x1t_free[p] = E("vector", rms2, waits=[W(cSC, s2), W(dCONST, const_done)],
                        inc=cVE)

        def ladder(e, t=tb):
            g_ = ST2[:, gamcol + t:gamcol + t + 1]
            e.tensor_scalar(fp(6144, 2048), fp(4096, 2048), g_, 128.5,
                            op0=OP.mult, op1=OP.add)
            bufs = [fp(6144, 2048), fp(10240, 2048), fp(12288, 2048)]
            cur = 0
            for b_ in range(7, -1, -1):
                s_ = bufs[cur]
                d_ = bufs[1] if cur != 1 else bufs[2]
                e.drain()
                e.tensor_scalar(fp(8192, 2048), s_, float(2 ** b_),
                                float(2 ** b_), op0=OP.is_ge, op1=OP.mult)
                e.drain()
                e.tensor_tensor(d_, s_, fp(8192, 2048), op=OP.subtract)
                cur = 1 if cur != 1 else 2
            rb_ = bufs[cur]
            e.drain()
            return e.scalar_tensor_tensor(fp(8192, 2048), fp(6144, 2048),
                                          128.0, rb_, op0=OP.subtract,
                                          op1=OP.subtract)
        lq = E("vector", ladder, inc=cVE)
        prev_ladder = lq
        for i in range(NCH):
            par = i % 2
            tr = E("tensor", lambda e, i_=i, par_=par: e.transpose(
                PS[:, 512 * par_:512 * par_ + 128],
                fp(8192 + 128 * i_, 128), identf[:]),
                waits=[W(cVE, lq), W(cSC, pt4_free[par]),
                       W(cVE, o_ps_free), W(cSC, s_banks_free)], inc=cTE)
            pt4_free[par] = E("scalar", lambda e, i_=i, t=tb, par_=par: e.activation(
                bf(1024 * i_ + 128 * t, 128), PS[:, 512 * par_:512 * par_ + 128],
                AF.Copy), waits=[W(cTE, tr)], inc=cSC)
    xq_done = cSC.v

    gu_ld_free = [W(cVE, ln1_last_fin)] * 4
    sg_scrap_free = [W(cTE, p3_last_mm)] * 4
    hu_free = [0, 0]
    gu_ps_free = 0
    gu_ps_free_sc = 0
    st_reuse_ok = ms2
    for op_ in range(11):
        tq_ready = []
        for i in range(NCH):
            pg = i % 2
            bufg = fp(14336 + 512 * pg, 512)
            ldg = dma("sync", bufg,
                      wguT[128 * i:128 * (i + 1), 512 * op_:512 * (op_ + 1)],
                      dGU[pg], waits=[gu_ld_free[pg], W(dSC2, scb_done)])

            def mk_sg(e, b=bufg, sid=pg):
                e.activation(bf(32768 + 1024 * sid, 512), b, AF.Sign,
                             bias=SCb[:, 1:2])
                return e.activation(bf(32768 + 1024 * sid + 512, 512), b,
                                    AF.Sign, bias=SCb[:, 0:1])
            sg = E("scalar", mk_sg,
                   waits=[W(dGU[pg], ldg), sg_scrap_free[pg]], inc=cSC)
            gu_ld_free[pg] = W(cSC, sg)
            tg = E("vector", lambda e, sid=pg, i_=i: e.tensor_tensor(
                bf(16384 + 512 * i_, 512), bf(32768 + 1024 * sid, 512),
                bf(32768 + 1024 * sid + 512, 512), op=OP.add),
                waits=[W(cSC, sg)], inc=cVE)
            sg_scrap_free[pg] = W(cVE, tg)
            pu = 2 + (i % 2)
            bufu = fp(14336 + 512 * pu, 512)
            ldu = dma("sync", bufu,
                      wguT[128 * i:128 * (i + 1), FF + 512 * op_:FF + 512 * (op_ + 1)],
                      dGU[pu], waits=[gu_ld_free[pu]])

            def mk_su(e, b=bufu, sid=pu):
                e.activation(bf(32768 + 1024 * sid, 512), b, AF.Sign,
                             bias=SCb[:, 1:2])
                return e.activation(bf(32768 + 1024 * sid + 512, 512), b,
                                    AF.Sign, bias=SCb[:, 0:1])
            su = E("scalar", mk_su,
                   waits=[W(dGU[pu], ldu), sg_scrap_free[pu]], inc=cSC)
            gu_ld_free[pu] = W(cSC, su)
            tu = E("vector", lambda e, sid=pu, i_=i: e.tensor_tensor(
                bf(24576 + 512 * i_, 512), bf(32768 + 1024 * sid, 512),
                bf(32768 + 1024 * sid + 512, 512), op=OP.add),
                waits=[W(cSC, su)], inc=cVE)
            sg_scrap_free[pu] = W(cVE, tu)
            tq_ready.append(cVE.v)
        for tbh in range(2):
            def mmgu(e, tbh_=tbh):
                for i in range(NCH):
                    for t4 in range(4):
                        tb = 4 * tbh_ + t4
                        lh = bf(1024 * i + 128 * tb, 128)
                        e.matmul(bank(t4), lh, bf(16384 + 512 * i, 512),
                                 start=(i == 0), stop=(i == NCH - 1))
                        mm = e.matmul(bank(4 + t4), lh, bf(24576 + 512 * i, 512),
                                      start=(i == 0), stop=(i == NCH - 1))
                return mm
            gu_mm = E("tensor", mmgu,
                      waits=[W(cVE, tq_ready[-1]), W(cVE, gu_ps_free),
                             W(cSC, gu_ps_free_sc)] +
                            ([W(cSC, xq_done)] if op_ == 0 else []), inc=cTE)
            rs_scrap_free = [gu_ps_free, gu_ps_free]
            for t4 in range(4):
                tb = 4 * tbh + t4
                p = t4 % 2
                hu = fp(8192 + 512 * p, 512)
                rl = fp(6144 + 512 * p, 512)
                sq_ = fp(7168 + 512 * p, 512)

                def mk_rs(e, t4_=t4, rl_=rl, s_=sq_):
                    e.activation(rl_, bank(t4_), AF.Relu)
                    e.drain()
                    return e.activation(s_, rl_, AF.Square)
                rs = E("scalar", mk_rs,
                       waits=[W(cTE, gu_mm), W(cVE, rs_scrap_free[p])], inc=cSC)
                if t4 >= 2:
                    gu_ps_free_sc = rs

                def mk_hu(e, t4_=t4, h_=hu, s_=sq_, col=tb * 11 + op_):
                    e.tensor_tensor(h_, s_, bank(4 + t4_), op=OP.mult)
                    e.drain()
                    return e.reduce_max(ST[:, col:col + 1], h_, axis=AX,
                                        apply_absolute_value=True)
                hv = E("vector", mk_hu,
                       waits=[W(cSC, rs), W(dHUS[p], hu_free[p]),
                              W(cVE, st_reuse_ok)], inc=cVE)
                rs_scrap_free[p] = hv
                if t4 >= 2:
                    gu_ps_free = hv
                hu_free[p] = dma("sync", sHU[128 * tb:128 * (tb + 1),
                                             512 * op_:512 * (op_ + 1)],
                                 hu, dHUS[p], waits=[W(cVE, hv)])
    p4a_hus_final = (dHUS[0].v, dHUS[1].v)

    def hstats(e):
        for tb in range(8):
            e.drain()
            e.reduce_max(ST2[:, 41:42], ST[:, 11 * tb:11 * tb + 11], axis=AX)
            e.drain()
            e.tensor_copy(ST2[:, 42:43], ST2[:, 41:42])
            e.drain()
            e.tensor_scalar(ST2[:, 43:44], ST2[:, gamcol + tb:gamcol + tb + 1],
                            SCb[:, 2:3], None, op0=OP.mult)
            e.drain()
            e.reciprocal(ST2[:, 44:45], ST2[:, 43:44])
            e.drain()
            e.tensor_tensor(ST2[:, 45:46], ST2[:, 44:45], ST2[:, 44:45], op=OP.mult)
            e.drain()
            e.tensor_tensor(ST2[:, 46:47], ST2[:, 45:46], ST2[:, 44:45], op=OP.mult)
            e.drain()
            e.tensor_tensor(ST2[:, 47:48], ST2[:, 46:47], ST2[:, 42:43], op=OP.mult)
            e.drain()
            e.tensor_scalar(ST2[:, 41:42], ST2[:, 47:48], 1e-5, None, op0=OP.max)
            e.drain()
            e.reciprocal(ST2[:, 47:48], ST2[:, 41:42])
            e.drain()
            e.tensor_tensor(ST2[:, 44:45], ST2[:, 47:48], ST2[:, 46:47], op=OP.mult)
            e.drain()
            e.tensor_scalar(ST2[:, 48 + tb:49 + tb], ST2[:, 44:45], 127.0, None,
                            op0=OP.mult)
            e.drain()
            e.tensor_scalar(ST2[:, 56 + tb:57 + tb], ST2[:, 41:42], SCb[:, 3:4],
                            float(1.0 / 127.0), op0=OP.mult, op1=OP.mult)
        return e.drain()
    hst = E("vector", hstats,
            waits=[W(dHUS[0], p4a_hus_final[0]), W(dHUS[1], p4a_hus_final[1]),
                   W(dSC2, scb_done)], inc=cVE)

    hu_ld_free = [hst, hst]
    pt4_free = [cSC.v, cSC.v]
    prev_seg_copies = cSC.v
    segs = [(0, 2048), (2048, 2048), (4096, 1536)]
    for tb in range(8):
        for s0, sn in segs:
            p = (s0 // 2048) % 2
            hub = fp(2048 * p, sn)
            ld = dma("sync", hub, sHU[128 * tb:128 * (tb + 1), s0:s0 + sn],
                     dHUL[p], waits=[W(cVE, hu_ld_free[p])])

            def hladder(e, h_=hub, t=tb, sn_=sn):
                g_ = ST2[:, 48 + t:49 + t]
                e.tensor_scalar(fp(4096, sn_), h_, g_, 128.5, op0=OP.mult,
                                op1=OP.add)
                bufs = [fp(4096, sn_), fp(8192, sn_), fp(10240, sn_)]
                cur = 0
                for b_ in range(7, -1, -1):
                    s_ = bufs[cur]
                    d_ = bufs[1] if cur != 1 else bufs[2]
                    e.drain()
                    e.tensor_scalar(fp(6144, sn_), s_, float(2 ** b_),
                                    float(2 ** b_), op0=OP.is_ge, op1=OP.mult)
                    e.drain()
                    e.tensor_tensor(d_, s_, fp(6144, sn_), op=OP.subtract)
                    cur = 1 if cur != 1 else 2
                rb_ = bufs[cur]
                e.drain()
                e.tensor_scalar(fp(12288, sn_), rb_, 0.0, None, op0=OP.is_le)
                e.drain()
                e.tensor_tensor(fp(14336, sn_), fp(12288, sn_), fp(6144, sn_),
                                op=OP.mult)
                e.drain()
                e.tensor_tensor(fp(12288, sn_), rb_, fp(14336, sn_), op=OP.add)
                e.drain()
                return e.scalar_tensor_tensor(fp(6144, sn_), fp(4096, sn_),
                                              128.0, fp(12288, sn_),
                                              op0=OP.subtract, op1=OP.subtract)
            hu_ld_free[p] = E("vector", hladder,
                              waits=[W(dHUL[p], ld), W(cSC, prev_seg_copies)],
                              inc=cVE)
            lq = cVE.v
            for k in range(sn // 128):
                fc = (s0 + 128 * k) // 128
                par = k % 2
                tr = E("tensor", lambda e, k_=k, par_=par: e.transpose(
                    PS[:, 512 * par_:512 * par_ + 128],
                    fp(6144 + 128 * k_, 128), identf[:]),
                    waits=[W(cVE, lq), W(cSC, pt4_free[par]), W(cVE, gu_ps_free)],
                    inc=cTE)
                pt4_free[par] = E("scalar", lambda e, fc_=fc, t=tb, par_=par:
                                  e.activation(bf(1024 * fc_ + 128 * t, 128),
                                               PS[:, 512 * par_:512 * par_ + 128],
                                               AF.Copy),
                                  waits=[W(cTE, tr)], inc=cSC)
            prev_seg_copies = cSC.v
    hqT_done = cSC.v

    td_ld_free = [W(cSC, hqT_done), W(cSC, hqT_done)]
    sd_scrap_free = [W(cSC, hqT_done), W(cSC, hqT_done)]
    x1s_free = [W(cSC, hqT_done), W(cSC, hqT_done)]
    ev_free2 = [0, 0]
    r2_tb_done = [0] * 8
    dn_ps_free = W(cSC, max(pt4_free))
    for ot in range(4):
        mm_prev = 0
        for fc in range(FF // 128):
            p = fc % 2
            buf = fp(512 * p, 512)
            ld = dma("sync", buf, wdT[128 * fc:128 * (fc + 1), 512 * ot:512 * (ot + 1)],
                     dTD[p], waits=[td_ld_free[p]])

            def mk_sd(e, b=buf, sid=p):
                e.activation(bf(46080 + 1024 * sid, 512), b, AF.Sign,
                             bias=SCb[:, 5:6])
                return e.activation(bf(46080 + 1024 * sid + 512, 512), b,
                                    AF.Sign, bias=SCb[:, 4:5])
            sd = E("scalar", mk_sd,
                   waits=[W(dTD[p], ld), sd_scrap_free[p]], inc=cSC)
            td_ld_free[p] = W(cSC, sd)
            tdq = E("vector", lambda e, sid=p: e.tensor_tensor(
                bf(45056 + 512 * sid, 512), bf(46080 + 1024 * sid, 512),
                bf(46080 + 1024 * sid + 512, 512), op=OP.add),
                waits=[W(cSC, sd), W(cTE, mm_prev)], inc=cVE)
            sd_scrap_free[p] = W(cVE, tdq)

            def mmd(e, fc_=fc, p_=p):
                for tb in range(8):
                    lh = bf(1024 * fc_ + 128 * tb, 128)
                    mm = e.matmul(bank(tb), lh, bf(45056 + 512 * p_, 512),
                                  start=(fc_ == 0), stop=(fc_ == FF // 128 - 1))
                return mm
            mm_prev = E("tensor", mmd,
                        waits=[W(cVE, cVE.v), dn_ps_free],
                        inc=cTE)
        for tb in range(8):
            p = tb % 2
            x1s = fp(1024 + 512 * p, 512)
            ld = dma("sync", x1s, sX1[128 * tb:128 * (tb + 1), 512 * ot:512 * (ot + 1)],
                     dXSL[p], waits=[x1s_free[p]])
            evb = fp(2048 + 512 * p, 512)
            x1s_free[p] = W(cVE, E("vector", lambda e, t=tb, x=x1s, ev=evb: e.scalar_tensor_tensor(
                ev, bank(t), ST2[:, 56 + t:57 + t], x, op0=OP.mult, op1=OP.add),
                waits=[W(cTE, mm_prev), W(dXSL[p], ld), W(dR2S[p], ev_free2[p])],
                inc=cVE))
            if tb >= 6:
                dn_ps_free = W(cVE, cVE.v)
            ev_free2[p] = dma("sync", sR2[128 * tb:128 * (tb + 1), 512 * ot:512 * (ot + 1)],
                              evb, dR2S[p], waits=[x1s_free[p]])
            if ot == 3:
                r2_tb_done[tb] = ev_free2[p]
    p4c_r2s_final = (dR2S[0].v, dR2S[1].v)

    rt_free2 = [0, 0]
    out_free = [0, 0]
    for tb in range(8):
        p = tb % 2
        rt = fp(3072 + 2048 * p, 2048)
        ld = dma("sync", rt, sR2[128 * tb:128 * (tb + 1), :], dR2L[p],
                 waits=[W(dR2S[tb % 2], r2_tb_done[tb]), W(cVE, rt_free2[p])])

        def ln_stats2(e, r=rt):
            e.reduce_sum(ST2[:, 28:29], r, axis=AX)
            e.drain()
            e.tensor_scalar(ST2[:, 29:30], ST2[:, 28:29], -(2.0 ** -11), None,
                            op0=OP.mult)
            return e.drain()
        lm = E("vector", ln_stats2, waits=[W(dR2L[p], ld)], inc=cVE)
        sq = E("scalar", lambda e, r=rt: e.activation(
            fp(11264, 2048), r, AF.Square, bias=ST2[:, 29:30],
            accum_out=ST2[:, 30:31]), waits=[W(cVE, lm)], inc=cSC)

        def ln_rstd2(e):
            e.tensor_scalar(ST2[:, 31:32], ST2[:, 30:31], 2.0 ** -11, 1e-5,
                            op0=OP.mult, op1=OP.add)
            return e.drain()
        v1_ = E("vector", ln_rstd2, waits=[W(cSC, sq)], inc=cVE)
        sq2 = E("scalar", lambda e: e.activation(ST2[:, 32:33], ST2[:, 31:32],
                                                 AF.Sqrt), waits=[W(cVE, v1_)],
                inc=cSC)

        def ln_fin2(e, r=rt, p_=p):
            e.reciprocal(ST2[:, 33:34], ST2[:, 32:33])
            e.drain()
            e.tensor_scalar(fp(13312, 2048), r, ST2[:, 29:30], None, op0=OP.add)
            e.drain()
            return e.tensor_scalar(fp(7168 + 2048 * p_, 2048), fp(13312, 2048),
                                   ST2[:, 33:34], None, op0=OP.mult)
        fin = E("vector", ln_fin2, waits=[W(cSC, sq2), W(dOUT[p], out_free[p])],
                inc=cVE)
        rt_free2[p] = fin
        out_free[p] = dma("sync", out[128 * tb:128 * (tb + 1), :],
                          fp(7168 + 2048 * p, 2048), dOUT[p], waits=[W(cVE, fin)])

    finals = [(c.idx, c.v) for c in all_ctrs if c.v > 0]

    nsem = Ctr._next[0]
    with ExitStack() as ctx:
        sems = [ctx.enter_context(nc.semaphore(f"sem{i}")) for i in range(nsem)]
        block = ctx.enter_context(nc.Block())

        def runner(eng, evs, tail=None, auto_drain=False):
            for waits, fn, (inci, incn) in evs:
                if auto_drain:
                    eng.drain()
                for (si, v) in waits:
                    if v > 0:
                        eng.wait_ge(sems[si], v)
                inst = fn(eng)
                if inci is not None:
                    inst.then_inc(sems[inci], incn)
            if tail:
                tail(eng)

        def sync_tail(e):
            for (si, v) in finals:
                e.wait_ge(sems[si], v)

        @block.sync
        def _(sync):
            runner(sync, events["sync"], tail=sync_tail)

        @block.tensor
        def _(tensor):
            runner(tensor, events["tensor"])

        @block.vector
        def _(vector):
            runner(vector, events["vector"], auto_drain=True)

        @block.scalar
        def _(scalar):
            runner(scalar, events["scalar"], auto_drain=True)

        @block.gpsimd
        def _(gpsimd):
            runner(gpsimd, events["gpsimd"])

    return nc


_CACHE = {}


def _host_inputs(x, w_q, w_k, w_v, w_o, w_gu, w_d, nw):
    in_maps = []
    own_all = []
    for c in range(8):
        b = c // 2
        slots = SLOTS_A if c % 2 == 0 else SLOTS_B
        own = np.concatenate([np.arange(128 * p_, 128 * (p_ + 1)) for p_ in slots])
        own_all.append((b, own))
        xTb = np.ascontiguousarray(x[b].T)
        mk = np.empty((128, 8 * 256), np.float32)
        for g, p_ in enumerate(slots):
            L = L_PROG[g]
            keys = (L - 2) * 128 + np.arange(256)
            qpos = 128 * p_ + np.arange(128)
            mk[:, 256 * g:256 * (g + 1)] = np.where(
                keys[None, :] <= qpos[:, None], np.float32(0.0), np.float32(-1e30))
        in_maps.append({
            "xT": xTb,
            "xTq": np.ascontiguousarray(xTb[:, own]),
            "xown": np.ascontiguousarray(x[b][own]),
            "wq": w_q, "wk": w_k, "wv": w_v, "wo": w_o,
            "wguT": np.ascontiguousarray(w_gu.T),
            "wdT": np.ascontiguousarray(w_d.T),
            "nw_in": np.ascontiguousarray(nw.reshape(1, D)),
            "identb_in": np.eye(128, dtype=ml_dtypes.bfloat16),
            "identf_in": np.eye(128, dtype=np.float32),
            "masks_in": mk,
        })
    return in_maps, own_all


def kernel(L0_attn_x, L0_attn_w_q, L0_attn_w_k, L0_attn_w_v, L0_attn_w_o,
           L0_ffn_w_gate_up, L0_ffn_w_down, L0_ffn_norm_weight):
    x = np.ascontiguousarray(np.asarray(L0_attn_x, dtype=np.float32))
    in_maps, own_all = _host_inputs(
        x,
        np.ascontiguousarray(np.asarray(L0_attn_w_q, np.float32)),
        np.ascontiguousarray(np.asarray(L0_attn_w_k, np.float32)),
        np.ascontiguousarray(np.asarray(L0_attn_w_v, np.float32)),
        np.ascontiguousarray(np.asarray(L0_attn_w_o, np.float32)),
        np.ascontiguousarray(np.asarray(L0_ffn_w_gate_up, np.float32)),
        np.ascontiguousarray(np.asarray(L0_ffn_w_down, np.float32)),
        np.ascontiguousarray(np.asarray(L0_ffn_norm_weight, np.float32)),
    )
    if "nc" not in _CACHE:
        _CACHE["nc"] = build_program()
    res = run_bass_kernel_spmd(_CACHE["nc"], in_maps, core_ids=list(range(8)))
    y = np.empty((B, S, D), dtype=np.asarray(res.results[0]["out"]).dtype)
    for c in range(8):
        b, own = own_all[c]
        y[b, own, :] = res.results[c]["out"]
    return y
